# version 12
# baseline (speedup 1.0000x reference)
import numpy as np
import ml_dtypes
import concourse.bass as bass
import concourse.mybir as mybir
from concourse.ap import AP
from concourse.bass_utils import run_bass_kernel_spmd

F32 = mybir.dt.float32
BF16 = mybir.dt.bfloat16
AF = mybir.ActivationFunctionType
ALU = mybir.AluOpType
AX = mybir.AxisListType
BF16NP = ml_dtypes.bfloat16

N_CORES = 8
B, T, D = 16, 2048, 1024
SEQ_PER_CORE = B // N_CORES
NTOK = SEQ_PER_CORE * T
NTILE = NTOK // 128
TPS = T // 128
NEG = -1e30

ENG_SECTIONS = {"pe": "tensor", "dve": "vector", "act": "scalar",
                "pool": "gpsimd", "sp": "sync"}
DMA_QUEUES = {"sp": "qsp", "act": "qact", "pool": "qpool"}


class Op:
    __slots__ = ("idx", "eng", "emit", "deps", "dma", "sem", "val", "tag", "inc")

    def __init__(self, idx, eng, emit, deps, dma, tag):
        self.idx, self.eng, self.emit, self.deps, self.dma, self.tag = \
            idx, eng, emit, deps, dma, tag
        self.sem = None
        self.val = None
        self.inc = False


class Prog:
    def __init__(self, nc):
        self.nc = nc
        self.ops = []
        self.res = {}
        self._uid = 0
        self.fence_deps = set()
        self._scopes = []
        self._consts = {}
        self.queue_dmas = {}
        self.QDEPTH = 4

    def uid(self, p):
        self._uid += 1
        return f"{p}_{self._uid}"

    def sb(self, prefix, shape, dtype):
        cm = self.nc.sbuf_tensor(self.uid(prefix), list(shape), dtype)
        h = cm.__enter__()
        if self._scopes:
            self._scopes[-1].append(cm)
        return h

    def psum(self, prefix, shape, dtype=F32):
        cm = self.nc.psum_tensor(self.uid(prefix), list(shape), dtype)
        h = cm.__enter__()
        if self._scopes:
            self._scopes[-1].append(cm)
        return h

    def constf(self, val, dtype=F32):
        ck = ("constf", float(val), dtype)
        if ck not in self._consts:
            cm = self.nc.sbuf_tensor(self.uid("cst"), [128, 1], dtype, side="right")
            h = cm.__enter__()
            key = ("cst", float(val))
            self.op("pool", lambda e, a=h, v=float(val): e.memset(a[:], v), w=[key])
            self._consts[ck] = (h[:, 0:1], key)
        return self._consts[ck]

    def push_scope(self):
        self._scopes.append([])

    def pop_scope(self):
        cms = self._scopes.pop()
        for cm in reversed(cms):
            cm.__exit__(None, None, None)
        self.barrier()

    def op(self, eng, emit, r=(), w=(), a=(), dma=False, tag="", nofence=False):
        idx = len(self.ops)
        deps = set() if nofence else set(self.fence_deps)
        for key in r:
            st = self.res.get(key)
            if st is not None and st[0] is not None:
                deps.add(st[0])
        for key in w:
            st = self.res.get(key)
            if st is not None:
                if st[0] is not None:
                    deps.add(st[0])
                deps.update(st[1])
        deps.discard(idx)
        for key in r:
            st = self.res.setdefault(key, [None, []])
            st[1].append(idx)
        for key in w:
            self.res[key] = [idx, []]
        for key in a:
            st = self.res.setdefault(key, [None, []])
            st[0] = idx
        o = Op(idx, eng, emit, deps, dma, tag)
        if dma:
            q = DMA_QUEUES[eng]
            lst = self.queue_dmas.setdefault(q, [])
            i = len(lst)
            if i >= self.QDEPTH:
                deps.add(lst[i - self.QDEPTH])
            lst.append(idx)
            o.sem = f"{q}{i % self.QDEPTH}"
            o.val = 16 * (i // self.QDEPTH + 1)
            o.inc = True
        self.ops.append(o)
        return o

    def barrier(self):
        last = {}
        for o in self.ops:
            key = ("q", o.sem) if o.dma else ("e", o.eng)
            last[key] = o.idx
        self.fence_deps = set(last.values())

    def dma(self, qeng, out_ap, in_ap, r=(), w=(), tag="dma", nofence=False):
        return self.op(qeng, lambda e, o=out_ap, i=in_ap: e.dma_start(out=o, in_=i),
                       r=r, w=w, dma=True, tag=tag, nofence=nofence)

    def mm(self, out_ap, lhsT_ap, rhs_ap, start, stop, r=(), w=(), a=(), tag="mm",
           tile_position=None):
        return self.op("pe", lambda e, o=out_ap, l=lhsT_ap, rr=rhs_ap, s1=start,
                       s2=stop, tp=tile_position:
                       e.matmul(o, l, rr, start=s1, stop=s2, tile_position=tp),
                       r=r, w=w, a=a, tag=tag)

    def mm_group(self, out_ap, prods, wkey, tag="mmg"):
        n = len(prods)
        for i, (l, rr, rk) in enumerate(prods):
            self.mm(out_ap, l, rr, i == 0, i == n - 1, r=rk,
                    w=([wkey] if i == 0 else ()), a=([wkey] if i > 0 else ()),
                    tag=tag)

    def tr(self, out_ap, in_ap, ident_ap, r=(), w=(), a=(), tag="tr"):
        return self.op("pe", lambda e, o=out_ap, i=in_ap, d=ident_ap:
                       e.transpose(o, i, d), r=list(r) + ["ident"], w=w, a=a, tag=tag)

    def copy(self, eng, out_ap, in_ap, r=(), w=(), tag="cp"):
        return self.op(eng, lambda e, o=out_ap, i=in_ap: e.tensor_copy(out=o, in_=i),
                       r=r, w=w, tag=tag)

    def tt(self, out_ap, in0, in1, alu, r=(), w=(), tag="tt", eng="dve"):
        return self.op(eng, lambda e, o=out_ap, a0=in0, a1=in1, op=alu:
                       e.tensor_tensor(out=o, in0=a0, in1=a1, op=op), r=r, w=w, tag=tag)

    def ts(self, out_ap, in0, s1, s2, op0, op1, r=(), w=(), tag="ts", eng="dve"):
        return self.op(eng, lambda e, o=out_ap, a0=in0, x1=s1, x2=s2, p0=op0, p1=op1:
                       e.tensor_scalar(out=o, in0=a0, scalar1=x1, scalar2=x2,
                                       op0=p0, op1=p1), r=r, w=w, tag=tag)

    def tsm(self, out_ap, in0, s1, r=(), w=(), tag="tsm"):
        return self.op("dve", lambda e, o=out_ap, a0=in0, x1=s1:
                       e.tensor_scalar_mul(out=o, in0=a0, scalar1=x1), r=r, w=w, tag=tag)

    def stt(self, out_ap, in0, scal, in1, op0, op1, r=(), w=(), tag="stt"):
        return self.op("dve", lambda e, o=out_ap, a0=in0, s=scal, a1=in1, p0=op0, p1=op1:
                       e.scalar_tensor_tensor(out=o, in0=a0, scalar=s, in1=a1,
                                              op0=p0, op1=p1), r=r, w=w, tag=tag)

    def red(self, out_ap, in_ap, alu, negate=False, r=(), w=(), tag="red", eng="dve"):
        return self.op(eng, lambda e, o=out_ap, i=in_ap, op=alu, n=negate:
                       e.tensor_reduce(out=o, in_=i, axis=AX.X, op=op, negate=n),
                       r=r, w=w, tag=tag)

    def recip(self, out_ap, in_ap, r=(), w=(), tag="rcp"):
        return self.op("dve", lambda e, o=out_ap, i=in_ap: e.reciprocal(out=o, in_=i),
                       r=r, w=w, tag=tag)

    def act(self, out_ap, in_ap, func, bias=0.0, scale=1.0, accum_out=None,
            r=(), w=(), tag="act"):
        return self.op("act", lambda e, o=out_ap, i=in_ap, f=func, b=bias, s=scale,
                       ac=accum_out: e.activation(out=o, in_=i, func=f, bias=b,
                                                  scale=s, accum_out=ac),
                       r=r, w=w, tag=tag)

    def acopy(self, out_ap, in_ap, scale=1.0, r=(), w=(), tag="acp"):
        return self.op("act", lambda e, o=out_ap, i=in_ap, s=scale:
                       e.activation(out=o, in_=i, func=AF.Copy, scale=s),
                       r=r, w=w, tag=tag)

    def emit_all(self):
        nc = self.nc
        awaited = set()
        for o in self.ops:
            awaited.update(o.deps)
        eng_count = {}
        queue_final = {}
        for o in self.ops:
            if o.dma:
                queue_final[o.sem] = o.val
            elif o.idx in awaited:
                eng_count[o.eng] = eng_count.get(o.eng, 0) + 1
                o.sem, o.val, o.inc = o.eng, eng_count[o.eng], True
        sem_names = sorted(set(list(eng_count) + list(queue_final)))
        sems = {s: nc.alloc_semaphore(f"sem_{s}") for s in sem_names}
        by_eng = {e: [] for e in ENG_SECTIONS}
        for o in self.ops:
            by_eng[o.eng].append(o)

        with nc.Block() as blk:
            for eng, section in ENG_SECTIONS.items():
                ops = by_eng[eng]
                if not ops:
                    continue

                def body(engine, ops=ops, eng=eng):
                    waited = {}
                    for o in ops:
                        targets = {}
                        for d in o.deps:
                            dop = self.ops[d]
                            if dop.sem is None:
                                continue
                            if targets.get(dop.sem, 0) < dop.val:
                                targets[dop.sem] = dop.val
                        for s, v in sorted(targets.items()):
                            if waited.get(s, 0) < v:
                                engine.wait_ge(sems[s], v)
                                waited[s] = v
                        inst = o.emit(engine)
                        if o.inc:
                            inst.then_inc(sems[o.sem], 16 if o.dma else 1)
                    q = DMA_QUEUES.get(eng)
                    if q:
                        for s, v in sorted(queue_final.items()):
                            if s.startswith(q) and waited.get(s, 0) < v:
                                engine.wait_ge(sems[s], v)

                getattr(blk, section)(body)
        return len(self.ops)


def fap(handle, offset, dims):
    base = handle.ap()
    part = list(base.ap)[0]
    return AP(base.tensor, offset, [list(part)] + [list(d) for d in dims])


def fm_cols(t_total, c, lo, n):
    return slice(c * t_total + lo, c * t_total + lo + n)



def load_ident(p, w):
    ident = p.sb("ident", [128, 128], BF16)
    p.dma("sp", ident[:], w["c_ident"], w=["ident"], tag="ld_ident")
    return ident


def emit_ln_to_fm(p, ident, x_dram_fn, tiles, xn_fm, xn_key, t_total,
                  lo_fm=None, lo_key=None, tag="ln",
                  normw_brd=None, xln_sb=None, xln_key=None,
                  per_tile_hook=None, pt=None):
    want_lo = lo_fm is not None
    want_f32 = want_lo or (normw_brd is not None)
    if pt is None:
        pt = [p.psum(f"{tag}pt", [128, 1024], BF16) for _ in range(2)]
    xt = [p.sb(f"{tag}xt", [128, 1024], F32) for _ in range(2)]
    st6 = [p.sb(f"{tag}s6", [128, 12], F32) for _ in range(2)]
    mv = [p.sb(f"{tag}mv", [128, 2], F32) for _ in range(2)]
    sd = [p.sb(f"{tag}sd", [128, 1], F32) for _ in range(2)]
    xnt = [p.sb(f"{tag}xb", [128, 1024], BF16) for _ in range(2)]
    xf32 = [p.sb(f"{tag}xf", [128, 1024], F32) for _ in range(2)] if want_f32 else None
    xlo = [p.sb(f"{tag}xl", [128, 1024], BF16) for _ in range(2)] if want_lo else None
    ms = [p.sb(f"{tag}ms", [128, 2], F32) for _ in range(2)] if normw_brd is not None else None

    for j, gi in tiles:
        pr = j % 2
        kxt, ks6, kmv, ksd, kxb = [f"{tag}{n}{pr}" for n in ("xt", "s6", "mv", "sd", "xb")]
        p.dma("sp", xt[pr][:], x_dram_fn(gi), w=[kxt], tag=f"{tag}ldx{gi}")
        p.op("dve", lambda e, a=st6[pr], b=xt[pr]: e.bn_stats(out=a[:, 0:6], in_=b[:, 0:512]),
             r=[kxt], w=[ks6])
        p.op("dve", lambda e, a=st6[pr], b=xt[pr]: e.bn_stats(out=a[:, 6:12], in_=b[:, 512:1024]),
             r=[kxt], w=[ks6])
        p.op("dve", lambda e, a=mv[pr], b=st6[pr]: e.bn_aggr(out=a[:, 0:2], in_=b[:]),
             r=[ks6], w=[kmv])
        eps_ap, eps_key = p.constf(1e-5)
        p.act(sd[pr][:], mv[pr][:, 1:2], AF.Sqrt, bias=eps_ap, r=[kmv, eps_key], w=[ksd])
        p.recip(sd[pr][:], sd[pr][:], r=[ksd], w=[ksd])
        if want_f32:
            kxf = f"{tag}xf{pr}"
            p.ts(xf32[pr][:], xt[pr][:], mv[pr][:, 0:1], sd[pr][:, 0:1],
                 ALU.subtract, ALU.mult, r=[kxt, kmv, ksd], w=[kxf])
            if normw_brd is not None:
                if xln_sb is not None:
                    p.copy("dve", xln_sb[:, 1024 * j:1024 * (j + 1)], xf32[pr][:],
                           r=[kxf], w=[(xln_key, j)])
                kms = f"{tag}ms{pr}"
                p.op("dve", lambda e, a=st6[pr], b=xf32[pr]: e.bn_stats(out=a[:, 0:6], in_=b[:, 0:512]),
                     r=[kxf], w=[ks6])
                p.op("dve", lambda e, a=st6[pr], b=xf32[pr]: e.bn_stats(out=a[:, 6:12], in_=b[:, 512:1024]),
                     r=[kxf], w=[ks6])
                p.op("dve", lambda e, a=ms[pr], b=st6[pr]: e.bn_aggr(out=a[:, 0:2], in_=b[:]),
                     r=[ks6], w=[kms])
                p.tt(ms[pr][:, 0:1], ms[pr][:, 0:1], ms[pr][:, 0:1], ALU.mult,
                     r=[kms], w=[kms])
                p.tt(ms[pr][:, 1:2], ms[pr][:, 1:2], ms[pr][:, 0:1], ALU.add,
                     r=[kms], w=[kms])
                eps2_ap, eps2_key = p.constf(1.1920929e-07)
                p.act(sd[pr][:], ms[pr][:, 1:2], AF.Sqrt, bias=eps2_ap,
                      r=[kms, eps2_key], w=[ksd])
                p.recip(sd[pr][:], sd[pr][:], r=[ksd], w=[ksd])
                p.stt(xnt[pr][:], xf32[pr][:], sd[pr][:, 0:1], normw_brd[:],
                      ALU.mult, ALU.mult, r=[kxf, ksd, "normw_brd"], w=[kxb])
            else:
                p.copy("dve", xnt[pr][:], xf32[pr][:], r=[kxf], w=[kxb])
        else:
            p.ts(xnt[pr][:], xt[pr][:], mv[pr][:, 0:1], sd[pr][:, 0:1],
                 ALU.subtract, ALU.mult, r=[kxt, kmv, ksd], w=[kxb])
        if want_lo:
            kxl = f"{tag}xl{pr}"
            p.tt(xlo[pr][:], xf32[pr][:], xnt[pr][:], ALU.subtract,
                 r=[f"{tag}xf{pr}", kxb], w=[kxl])
        srcs = [(xnt[pr], kxb, xn_fm, xn_key)]
        if want_lo:
            srcs.append((xlo[pr], f"{tag}xl{pr}", lo_fm, lo_key))
        for src, skey, dst, dkey in srcs:
            kbank = f"{tag}ptb{pr}"
            for c in range(8):
                p.tr(pt[pr][:, 128 * c:128 * (c + 1)], src[:, 128 * c:128 * (c + 1)],
                     ident[:], r=[skey], w=([kbank] if c == 0 else ()),
                     a=([kbank] if c > 0 else ()))
            dst_ap = fap(dst, 128 * j, [[t_total, 8], [1, 128]])
            p.copy("dve", dst_ap, pt[pr][:], r=[kbank], w=[(dkey, j)],
                   tag=f"{tag}evac{gi}")
        if per_tile_hook is not None:
            per_tile_hook(j, gi)


def emit_router(p, z_sb, zkey, col0, L, k, out_ap, out_key, scale_out, rsc, tag="rt"):
    zc = z_sb[:, col0:col0 + L]
    m1, mk, g, zk, zk2, cp, b3, sm = (rsc[n] for n in
                                      ("m1", "mk", "g", "zk", "zk2", "cp", "b3", "sum"))
    K = lambda n: f"{tag}{n}"
    p.red(m1[:], zc, ALU.max, r=[zkey], w=[K("m1")])
    cur, kcur = zc, [zkey]
    curm, kcurm = m1, K("m1")
    for step in range(k - 1):
        nxt, knxt = (zk, K("zk")) if step == 0 else (zk2, K("zk2"))
        p.ts(g[:, 0:L], cur, curm[:, 0:1], None, ALU.is_ge, ALU.bypass,
             r=kcur + [kcurm], w=[K("g")])
        p.stt(nxt[:, 0:L], g[:, 0:L], NEG, cur, ALU.mult, ALU.add,
              r=[K("g")] + kcur, w=[knxt])
        cur, kcur = nxt[:, 0:L], [knxt]
        p.red(mk[:], cur, ALU.max, r=kcur, w=[K("mk") + str(step)])
        curm, kcurm = mk, K("mk") + str(step)
    p.ts(g[:, 0:L], zc, curm[:, 0:1], None, ALU.is_ge, ALU.bypass,
         r=[zkey, kcurm], w=[K("g")])
    p.tsm(b3[:], m1[:], -1.0, r=[K("m1")], w=[K("b3")])
    p.act(cp[:, 0:L], zc, AF.Exp, bias=b3[:, 0:1], scale=1.0,
          r=[zkey, K("b3")], w=[K("cp")])
    p.tt(cp[:, 0:L], cp[:, 0:L], g[:, 0:L], ALU.mult, r=[K("cp"), K("g")], w=[K("cp")])
    p.red(sm[:], cp[:, 0:L], ALU.add, r=[K("cp")], w=[K("sum")])
    p.recip(sm[:], sm[:], r=[K("sum")], w=[K("sum")])
    if scale_out != 1.0:
        p.tsm(sm[:], sm[:], scale_out, r=[K("sum")], w=[K("sum")])
    p.ts(out_ap, cp[:, 0:L], sm[:, 0:1], None, ALU.mult, ALU.bypass,
         r=[K("cp"), K("sum")], w=[out_key])


def router_scratch(p, L):
    rsc = {}
    for n in ("g", "zk", "zk2", "cp"):
        rsc[n] = p.sb(f"r{n}", [128, L], F32)
    for n in ("m1", "mk", "b3", "sum"):
        rsc[n] = p.sb(f"r{n}", [128, 1], F32)
    return rsc



def build_l0_ffn(p, ident, w, x_in, x_out, uname=""):
    GT = 2048
    GTILES = GT // 128

    p.push_scope()
    pw = p.sb("pw", [128, 512], F32)
    p.dma("sp", pw[:], w["P_wT"].rearrange("(c p) a -> p c a", p=128),
          w=["pw"], tag="ld_pw")
    pwh = p.sb("pwh", [128, 512], BF16)
    pwl = p.sb("pwl", [128, 512], BF16)
    p.copy("dve", pwh[:], pw[:], r=["pw"], w=["pwh"])
    p.tt(pwl[:], pw[:], pwh[:], ALU.subtract, r=["pw", "pwh"], w=["pwl"])
    uc = p.sb("uc", [64, 36], F32)
    p.dma("sp", uc[:], w["U_cat"], w=["uc"], tag="ld_uc")
    uch = p.sb("uch", [64, 36], BF16)
    ucl = p.sb("ucl", [64, 36], BF16)
    p.copy("dve", uch[:], uc[:], r=["uc"], w=["uch"])
    p.tt(ucl[:], uc[:], uch[:], ALU.subtract, r=["uc", "uch"], w=["ucl"])
    b2 = p.sb("b2", [12, 1024], BF16)
    p.dma("pool", b2[:], w["b2"], w=["b2"], tag="ld_b2")

    for g in range(NTOK // GT):
        p.push_scope()
        xn = p.sb("xn", [128, 8 * GT], BF16)
        c1 = p.sb("c1", [128, 12 * GTILES], F32)
        c2 = p.sb("c2", [128, 12 * GTILES], F32)
        c3f = p.sb("c3f", [12, 128 * GTILES], BF16)

        p.push_scope()
        xn_lo = p.sb("xnlo", [128, 8 * GT], BF16)
        zps = p.psum("zps", [128, 64], F32)
        xaps = p.psum("xaps", [64, 128], F32)
        ptc3 = p.psum("ptc3", [128, 128], BF16)
        z_sb = p.sb("z", [128, 36], F32)
        xah = p.sb("xah", [64, 128], BF16)
        xal = p.sb("xal", [64, 128], BF16)
        rsc = router_scratch(p, 12)
        c3w = p.sb("c3w", [128, 12], F32)
        c3t = p.sb("c3t", [128, 12], BF16)

        def hook(j, gi, g=g, xn=xn, xn_lo=xn_lo, c1=c1, c2=c2, c3f=c3f,
                 zps=zps, xaps=xaps, ptc3=ptc3, z_sb=z_sb, xah=xah, xal=xal,
                 rsc=rsc, c3w=c3w, c3t=c3t):
            tg = f"{uname}g{g}t{j}"
            prods = []
            for c in range(8):
                xn_c = xn[:, fm_cols(GT, c, 128 * j, 128)]
                lo_c = xn_lo[:, fm_cols(GT, c, 128 * j, 128)]
                pc = pw_slice = slice(64 * c, 64 * (c + 1))
                prods.append((pwh[:, pc], xn_c, ["pwh", ("xnfm", j)]))
                prods.append((pwl[:, pc], xn_c, ["pwl", ("xnfm", j)]))
                prods.append((pwh[:, pc], lo_c, ["pwh", ("xnlo", j)]))
            p.mm_group(xaps[:], prods, "xaps", tag=f"xa{tg}")
            p.copy("dve", xah[:], xaps[:], r=["xaps"], w=["xah"])
            p.tt(xal[:], xaps[:], xah[:], ALU.subtract, r=["xaps", "xah"], w=["xal"])
            p.mm_group(zps[:, 0:36],
                       [(xah[:], uch[:], ["xah", "uch"]),
                        (xah[:], ucl[:], ["xah", "ucl"]),
                        (xal[:], uch[:], ["xal", "uch"])], "zps", tag=f"z{tg}")
            p.copy("dve", z_sb[:], zps[:, 0:36], r=["zps"], w=["zsb"])
            emit_router(p, z_sb, "zsb", 0, 12, 3, c1[:, 12 * j:12 * (j + 1)],
                        ("c1", j), 1.0, rsc, tag=f"{tg}r0")
            emit_router(p, z_sb, "zsb", 12, 12, 3, c2[:, 12 * j:12 * (j + 1)],
                        ("c2", j), 1.0, rsc, tag=f"{tg}r1")
            emit_router(p, z_sb, "zsb", 24, 12, 3, c3w[:, 0:12],
                        "c3w", 1.0, rsc, tag=f"{tg}r2")
            p.copy("dve", c3t[:], c3w[:], r=["c3w"], w=["c3t"])
            p.tr(ptc3[0:12, :], c3t[:, 0:12], ident[:], r=["c3t"], w=["ptc3"])
            p.copy("dve", c3f[:, 128 * j:128 * (j + 1)], ptc3[0:12, :],
                   r=["ptc3"], w=[("c3f", j)])

        emit_ln_to_fm(p, ident, lambda gi: x_in[128 * gi:128 * (gi + 1), :],
                      [(j, g * GTILES + j) for j in range(GTILES)],
                      xn, "xnfm", GT, lo_fm=xn_lo, lo_key="xnlo",
                      tag=f"{uname}lnf{g}_", per_tile_hook=hook)
        p.pop_scope()

        p.push_scope()
        acc = p.sb("acc", [128, 1024 * GTILES], F32)
        h_fm = p.sb("hfm", [128, 8 * GT], BF16)
        wsl = [p.sb("wsl", [128, 8 * 1024], BF16) for _ in range(2)]
        msp = [p.psum("msp", [128, 512], F32) for _ in range(2)]
        pth = [p.psum("pth", [128, 1024], BF16) for _ in range(2)]
        htile = [p.sb("ht", [128, 1024], BF16)]

        def w_fm_src(name, e):
            return w[name][e].rearrange("(c p) h -> p c h", p=128)

        def mix(src_fm, src_key, wname, cmat, ckey, init):
            for e in range(12):
                sl = e % 2
                kw = f"wslot{sl}"
                p.dma("pool", wsl[sl][:], w_fm_src(wname, e), w=[kw],
                      tag=f"ld{wname}{e}", nofence=(e >= 2))
                for j in range(GTILES):
                    for dg in range(2):
                        pi = (j * 2 + dg) % 2
                        ps, kps = msp[pi], f"msp{pi}"
                        prods = [(src_fm[:, fm_cols(GT, c, 128 * j, 128)],
                                  wsl[sl][:, 1024 * c + 512 * dg:1024 * c + 512 * (dg + 1)],
                                  [(src_key, j), kw]) for c in range(8)]
                        p.mm_group(ps[:], prods, kps, tag=f"{wname}e{e}j{j}")
                        dst = acc[:, 1024 * j + 512 * dg:1024 * j + 512 * (dg + 1)]
                        kacc = ("acc", j, dg)
                        cs = cmat[:, 12 * j + e:12 * j + e + 1]
                        if init and e == 0:
                            p.ts(dst, ps[:], cs, None, ALU.mult, ALU.bypass,
                                 r=[kps, (ckey, j)], w=[kacc])
                        else:
                            p.stt(dst, ps[:], cs, dst, ALU.mult, ALU.add,
                                  r=[kps, (ckey, j), kacc], w=[kacc])

        mix(xn, "xnfm", "W1T", c1, "c1", init=True)
        for j in range(GTILES):
            pr = j % 2
            p.act(htile[0][:], acc[:, 1024 * j:1024 * (j + 1)], AF.Gelu,
                  r=[("acc", j, 0), ("acc", j, 1)], w=["ht0"], tag=f"gelu{j}")
            kbank = f"pth{pr}"
            for c in range(8):
                p.tr(pth[pr][:, 128 * c:128 * (c + 1)],
                     htile[0][:, 128 * c:128 * (c + 1)], ident[:],
                     r=["ht0"], w=([kbank] if c == 0 else ()),
                     a=([kbank] if c > 0 else ()))
            p.copy("dve", fap(h_fm, 128 * j, [[GT, 8], [1, 128]]), pth[pr][:],
                   r=[kbank], w=[("hfm", j)])

        xt2 = [p.sb("xt2", [128, 1024], F32) for _ in range(2)]
        for j in range(GTILES):
            pr = j % 2
            kx = f"xt2{pr}"
            gi = g * GTILES + j
            p.dma("sp", xt2[pr][:], x_in[128 * gi:128 * (gi + 1), :], w=[kx],
                  tag=f"ldx2{gi}")
            for dg in range(2):
                pi = (j * 2 + dg) % 2
                ps, kps = msp[pi], f"msp{pi}"
                p.mm(ps[:], c3f[:, 128 * j:128 * (j + 1)],
                     b2[:, 512 * dg:512 * (dg + 1)], True, True,
                     r=[("c3f", j), "b2"], w=[kps], tag=f"b2j{j}")
                dst = acc[:, 1024 * j + 512 * dg:1024 * j + 512 * (dg + 1)]
                p.tt(dst, ps[:], xt2[pr][:, 512 * dg:512 * (dg + 1)], ALU.add,
                     r=[kps, kx], w=[("acc", j, dg)])

        mix(h_fm, "hfm", "W2T", c2, "c2", init=False)
        for j in range(GTILES):
            gi = g * GTILES + j
            p.dma("sp", x_out[128 * gi:128 * (gi + 1), :],
                  acc[:, 1024 * j:1024 * (j + 1)],
                  r=[("acc", j, 0), ("acc", j, 1)], tag=f"stx{gi}")
        p.pop_scope()
        p.pop_scope()
    p.pop_scope()


def l0_ffn_host_weights(inputs):
    return {
        "P_wT": np.ascontiguousarray(inputs["L0_ffn_P_w"].T),
        "U_cat": np.ascontiguousarray(np.concatenate(
            [inputs["L0_ffn_U1"].T, inputs["L0_ffn_U2"].T, inputs["L0_ffn_U3"].T],
            axis=1)),
        "W1T": np.ascontiguousarray(inputs["L0_ffn_W1"].transpose(0, 2, 1)),
        "W2T": np.ascontiguousarray(inputs["L0_ffn_W2"].transpose(0, 2, 1)),
        "b2": np.ascontiguousarray(inputs["L0_ffn_b2"]),
    }


WSPECS_L0_FFN = [("P_wT", [1024, 64]), ("U_cat", [64, 36]),
                 ("W1T", [12, 1024, 1024]), ("W2T", [12, 1024, 1024]),
                 ("b2", [12, 1024])]



class LnTilePool:

    def __init__(self, p, tag, rms=False):
        self.p, self.tag, self.rms = p, tag, rms
        self.xt = [p.sb(f"{tag}xt", [128, 1024], F32) for _ in range(2)]
        self.s6 = [p.sb(f"{tag}s6", [128, 12], F32) for _ in range(2)]
        self.mv = [p.sb(f"{tag}mv", [128, 2], F32) for _ in range(2)]
        self.sd = [p.sb(f"{tag}sd", [128, 1], F32) for _ in range(2)]
        self.xb = [p.sb(f"{tag}xb", [128, 1024], BF16) for _ in range(2)]
        self.xf = [p.sb(f"{tag}xf", [128, 1024], F32) for _ in range(2)] if rms else None
        self.ms = [p.sb(f"{tag}ms", [128, 2], F32) for _ in range(2)] if rms else None
        self.pt = [p.psum(f"{tag}pt", [128, 1024], BF16) for _ in range(2)]


def emit_ln_tile(p, ident, pool, j, x_ap, dst_fm_ap, dst_key, normw=None):
    pr = j % 2
    tag = pool.tag
    kxt, ks6, kmv, ksd, kxb = [f"{tag}{n}{pr}" for n in ("xt", "s6", "mv", "sd", "xb")]
    xt, s6, mv, sd, xb = (pool.xt[pr], pool.s6[pr], pool.mv[pr], pool.sd[pr],
                          pool.xb[pr])
    p.dma("sp", xt[:], x_ap, w=[kxt], tag=f"{tag}ldx")
    p.op("dve", lambda e, a=s6, b=xt: e.bn_stats(out=a[:, 0:6], in_=b[:, 0:512]),
         r=[kxt], w=[ks6])
    p.op("dve", lambda e, a=s6, b=xt: e.bn_stats(out=a[:, 6:12], in_=b[:, 512:1024]),
         r=[kxt], w=[ks6])
    p.op("dve", lambda e, a=mv, b=s6: e.bn_aggr(out=a[:, 0:2], in_=b[:]),
         r=[ks6], w=[kmv])
    eps_ap, eps_key = p.constf(1e-5)
    p.act(sd[:], mv[:, 1:2], AF.Sqrt, bias=eps_ap, r=[kmv, eps_key], w=[ksd])
    p.recip(sd[:], sd[:], r=[ksd], w=[ksd])
    if normw is None:
        p.ts(xb[:], xt[:], mv[:, 0:1], sd[:, 0:1], ALU.subtract, ALU.mult,
             r=[kxt, kmv, ksd], w=[kxb])
    else:
        xf, ms = pool.xf[pr], pool.ms[pr]
        kxf, kms = f"{tag}xf{pr}", f"{tag}ms{pr}"
        p.ts(xf[:], xt[:], mv[:, 0:1], sd[:, 0:1], ALU.subtract, ALU.mult,
             r=[kxt, kmv, ksd], w=[kxf])
        p.op("dve", lambda e, a=s6, b=xf: e.bn_stats(out=a[:, 0:6], in_=b[:, 0:512]),
             r=[kxf], w=[ks6])
        p.op("dve", lambda e, a=s6, b=xf: e.bn_stats(out=a[:, 6:12], in_=b[:, 512:1024]),
             r=[kxf], w=[ks6])
        p.op("dve", lambda e, a=ms, b=s6: e.bn_aggr(out=a[:, 0:2], in_=b[:]),
             r=[ks6], w=[kms])
        p.tt(ms[:, 0:1], ms[:, 0:1], ms[:, 0:1], ALU.mult, r=[kms], w=[kms])
        p.tt(ms[:, 1:2], ms[:, 1:2], ms[:, 0:1], ALU.add, r=[kms], w=[kms])
        eps2_ap, eps2_key = p.constf(1.1920929e-07)
        p.act(sd[:], ms[:, 1:2], AF.Sqrt, bias=eps2_ap, r=[kms, eps2_key], w=[ksd])
        p.recip(sd[:], sd[:], r=[ksd], w=[ksd])
        p.stt(xb[:], xf[:], sd[:, 0:1], normw[0][:], ALU.mult, ALU.mult,
              r=[kxf, ksd, normw[1]], w=[kxb])
    kbank = f"{tag}ptb{pr}"
    for c in range(8):
        p.tr(pool.pt[pr][:, 128 * c:128 * (c + 1)], xb[:, 128 * c:128 * (c + 1)],
             ident[:], r=[kxb], w=([kbank] if c == 0 else ()),
             a=([kbank] if c > 0 else ()))
    p.acopy(dst_fm_ap, pool.pt[pr][:], r=[kbank], w=[dst_key])


class PtPool:

    def __init__(self, p, tag):
        self.tag = tag
        self.pt = [p.psum(f"{tag}pt", [128, 1024], BF16) for _ in range(2)]


def transpose_tile_to(p, ident, pool, pr, src_sb, src_key, dst_ap, dst_key, nch=8):
    tag = pool.tag
    kbank = f"{tag}ptb{pr}"
    for c in range(nch):
        p.tr(pool.pt[pr][:, 128 * c:128 * (c + 1)], src_sb[:, 128 * c:128 * (c + 1)],
             ident[:], r=[src_key], w=([kbank] if c == 0 else ()),
             a=([kbank] if c > 0 else ()))
    p.acopy(dst_ap, pool.pt[pr][:, 0:128 * nch], r=[kbank], w=[dst_key])


def emit_attn_core(p, tag, ident, *, kind, suffix, QGT, nheads, q_ap, k_ap, v_ap,
                   out_base, attn_write, scale, maskv, TT_=2048):
    NQG = TPS // QGT
    QGW = QGT * 128
    sps = [p.psum(f"{tag}sps", [128, 512], F32) for _ in range(2)]
    ptp = [p.psum(f"{tag}ptp", [128, 1024], BF16) for _ in range(2)]
    pvp = [p.psum(f"{tag}pvp", [128, QGW], F32) for _ in range(2)]
    S_sb2 = [[p.sb(f"{tag}S", [128, TT_], F32) for _ in range(QGT)] for _ in range(2)]
    P_sb2 = [[p.sb(f"{tag}P", [128, TT_], BF16) for _ in range(QGT)] for _ in range(2)]
    PT_sb2 = [p.sb(f"{tag}PT", [128, (TT_ // 128) * QGW], BF16) for _ in range(2)]
    msc2 = [[p.sb(f"{tag}m", [128, 1], F32) for _ in range(QGT)] for _ in range(2)]
    lsc2 = [[p.sb(f"{tag}l", [128, 1], F32) for _ in range(QGT)] for _ in range(2)]
    cmx2 = [[p.sb(f"{tag}c", [128, 4], F32) for _ in range(QGT)] for _ in range(2)]
    iters = [(h, g) for h in range(nheads) for g in range(NQG)]

    def geom(g):
        f_own = (g * QGT) // 4
        if suffix:
            kcs = list(range(f_own, TT_ // 512))
            col0 = 512 * f_own
        else:
            kcs = list(range(0, f_own + 1))
            col0 = 0
        return kcs, col0, 512 * len(kcs)

    def stage_a(it):
        h, g = iters[it]
        pp = it % 2
        S_sb, msc, lsc, cmx = S_sb2[pp], msc2[pp], lsc2[pp], cmx2[pp]
        tagp = f"{tag}{pp}"
        kcs, col0, L = geom(g)
        for si in range(QGT):
            t128 = g * QGT + si
            lhs, lrk = q_ap(h, t128)
            kc_all = [(f"{tagp}S", si, kc) for kc in kcs]
            for ki, kc in enumerate(kcs):
                pi = (si + ki) % 2
                ps, kps = sps[pi], f"{tag}sps{pi}"
                kS = (f"{tagp}S", si, kc)
                rhs, rrk = k_ap(h, kc)
                p.mm(ps[:], lhs, rhs, True, True, r=lrk + rrk, w=[kps],
                     tag=f"{tag}sc")
                dst = S_sb[si][:, 512 * kc - col0:512 * kc - col0 + 512]
                partial = (kc == t128 // 4)
                if kind == "softmax":
                    if partial:
                        mp, mk = maskv[t128 % 4]
                        if isinstance(scale, tuple):
                            p.stt(dst, ps[:], scale[0], mp, ALU.mult, ALU.add,
                                  r=[kps, scale[1], mk], w=[kS])
                        else:
                            p.stt(dst, ps[:], float(scale), mp, ALU.mult,
                                  ALU.add, r=[kps, mk], w=[kS])
                    else:
                        sc = scale[0] if isinstance(scale, tuple) else float(scale)
                        rr = [kps, scale[1]] if isinstance(scale, tuple) else [kps]
                        p.acopy(dst, ps[:], scale=sc, r=rr, w=[kS])
                else:
                    if partial:
                        mp, mk = maskv[t128 % 4]
                        p.tt(dst, ps[:], mp, ALU.mult, r=[kps, mk], w=[kS])
                    else:
                        p.acopy(dst, ps[:], r=[kps], w=[kS])
                alu = ALU.max if kind == "softmax" else ALU.add
                p.red(cmx[si][:, ki:ki + 1], dst, alu, r=[kS],
                      w=[(f"{tagp}c", si)])
            km, kl = f"{tagp}m{si}", f"{tagp}l{si}"
            if kind == "softmax":
                p.red(msc[si][:], cmx[si][:, 0:len(kcs)], ALU.max, negate=True,
                      r=[(f"{tagp}c", si)], w=[km])
            else:
                p.red(lsc[si][:], cmx[si][:, 0:len(kcs)], ALU.add,
                      r=[(f"{tagp}c", si)], w=[kl])
                p.op("dve", lambda e, a=lsc[si]: e.tensor_scalar_add(
                     out=a[:], in0=a[:], scalar1=1e-6), r=[kl], w=[kl])

    def stage_b(it):
        h, g = iters[it]
        pp = it % 2
        S_sb, P_sb, PT_sb, msc, lsc = (S_sb2[pp], P_sb2[pp], PT_sb2[pp],
                                       msc2[pp], lsc2[pp])
        tagp = f"{tag}{pp}"
        kcs, col0, L = geom(g)
        nkt = L // 128
        for si in range(QGT):
            kS_all = [(f"{tagp}S", si, kc) for kc in kcs]
            km, kl = f"{tagp}m{si}", f"{tagp}l{si}"
            if kind == "softmax":
                p.op("act", lambda e, a=S_sb[si], b=msc[si], c=lsc[si], LL=L:
                     e.activation(out=a[:, 0:LL], in_=a[:, 0:LL], func=AF.Exp,
                                  bias=b[:, 0:1], scale=1.0, accum_out=c[:]),
                     r=kS_all + [km], w=kS_all + [kl])
            p.recip(lsc[si][:], lsc[si][:], r=[kl], w=[kl])
            p.acopy(P_sb[si][:, 0:L], S_sb[si][:, 0:L], scale=lsc[si][:, 0:1],
                    r=kS_all + [kl], w=[(f"{tagp}P", si)])
        for kbt in range(nkt):
            pi = kbt % 2
            kbank = f"{tag}ptpb{pi}"
            for si in range(QGT):
                p.tr(ptp[pi][:, 128 * si:128 * (si + 1)],
                     P_sb[si][:, 128 * kbt:128 * (kbt + 1)], ident[:],
                     r=[(f"{tagp}P", si)], w=([kbank] if si == 0 else ()),
                     a=([kbank] if si > 0 else ()))
            p.copy("dve", PT_sb[:, QGW * kbt:QGW * (kbt + 1)],
                   ptp[pi][:, 0:QGW], r=[kbank], w=[(f"{tagp}PT", kbt)])
        b = out_base(h)
        pi = it % 2
        pv, kpv = pvp[pi], f"{tag}pvp{pi}"
        tp = (0, 64) if b == 64 else None
        for kbt in range(nkt):
            vap, vrk = v_ap(h, col0 // 128 + kbt)
            p.mm(pv[b:b + 64, :], vap, PT_sb[:, QGW * kbt:QGW * (kbt + 1)],
                 kbt == 0, kbt == nkt - 1,
                 r=vrk + [(f"{tagp}PT", kbt)],
                 w=([kpv] if kbt == 0 else ()), a=([kpv] if kbt > 0 else ()),
                 tag=f"{tag}pv", tile_position=tp)
        attn_write(h, g, pv[b:b + 64, :], kpv)

    stage_a(0)
    for it in range(1, len(iters)):
        stage_a(it)
        stage_b(it - 1)
    stage_b(len(iters) - 1)


def emit_oproj(p, tag, ident, attn_fm, afm_rkeys, wo_sb, wo_key, x_in, x_out,
               seq0_tile, epilogue, QGT):
    ops = [p.psum(f"{tag}ops", [128, 512], F32) for _ in range(2)]
    xt2 = [p.sb(f"{tag}xt2", [128, 1024], F32) for _ in range(2)]
    ot = [p.sb(f"{tag}ot", [128, 1024], F32) for _ in range(2)]
    for j in range(TPS):
        gi = seq0_tile + j
        pr = j % 2
        kx, ko = f"{tag}xt2{pr}", f"{tag}ot{pr}"
        p.dma("sp", xt2[pr][:], x_in[128 * gi:128 * (gi + 1), :], w=[kx],
              tag=f"{tag}ldx2")
        for dg in range(2):
            pi = (2 * j + dg) % 2
            ps, kps = ops[pi], f"{tag}ops{pi}"
            prods = []
            for c in range(8):
                prods.append((attn_fm[:, 2048 * c + 128 * j:2048 * c + 128 * (j + 1)],
                              wo_sb[:, 1024 * c + 512 * dg:1024 * c + 512 * (dg + 1)],
                              afm_rkeys(c, j) + [wo_key]))
            p.mm_group(ps[:], prods, kps, tag=f"{tag}op")
            epilogue(j, dg, ps[:], kps, xt2[pr], kx,
                     ot[pr][:, 512 * dg:512 * (dg + 1)], ko)
        p.dma("sp", x_out[128 * gi:128 * (gi + 1), :], ot[pr][:],
              r=[ko], tag=f"{tag}stx")


def load_w_fm(p, qeng, dram_ap, dout, name_key, dtype=BF16):
    sb = p.sb("wfm", [128, 8 * dout], dtype)
    src = dram_ap.rearrange("(c p) d -> p c d", p=128)
    p.dma(qeng, sb[:], src, w=[name_key], tag=f"ld_{name_key}")
    return sb



def build_l0_attn_a(p, ident, w, x_in, x_out, uname=""):
    scl = p.sb("ascl", [128, 1], F32)
    p.dma("sp", scl[:], w["attn_scale_rep"], w=["ascl"], tag="ld_ascl")
    maskv = []
    for v in range(4):
        m = p.sb("amask", [128, 512], F32)
        p.dma("sp", m[:], w["maskv"][v * 128:(v + 1) * 128, :], w=[("amask", v)],
              tag="ld_mask")
        maskv.append((m[:], ("amask", v)))
    for s in range(SEQ_PER_CORE):
        p.push_scope()
        q_fm = p.sb("qfm", [128, 8 * T], BF16)
        k_fm = p.sb("kfm", [128, 8 * T], BF16)
        v_tok = p.sb("vtok", [128, TPS * 1024], BF16)
        s0 = s * TPS

        p.push_scope()
        pool = LnTilePool(p, f"{uname}qv{s}_")
        xnf = p.sb("xnf", [128, 8 * 128], BF16)
        qkps = [p.psum("qkps", [128, 512], F32) for _ in range(2)]
        qtk = [p.sb("qtk", [128, 1024], BF16) for _ in range(2)]
        wq = load_w_fm(p, "pool", w["WqT"], 1024, "wq")
        wk = load_w_fm(p, "pool", w["WkT"], 1024, "wk")
        wv = load_w_fm(p, "pool", w["WvT"], 1024, "wv")
        bqb = p.sb("bqb", [128, 1024], F32)
        p.dma("sp", bqb[:], w["bq_brd"], w=["bqb"], tag="ld_bq")
        bkb = p.sb("bkb", [128, 1024], F32)
        p.dma("sp", bkb[:], w["bk_brd"], w=["bkb"], tag="ld_bk")
        bvb = p.sb("bvb", [128, 1024], F32)
        p.dma("sp", bvb[:], w["bv_brd"], w=["bvb"], tag="ld_bv")
        biases = {"q": (bqb, "bqb"), "k": (bkb, "bkb"), "v": (bvb, "bvb")}
        for j in range(TPS):
            pr = j % 2
            emit_ln_tile(p, ident, pool, j,
                         x_in[128 * (s0 + j):128 * (s0 + j + 1), :],
                         xnf.ap(), ("xnf",))
            for wsb, wkey, bias in ((wq, "wq", "q"), (wk, "wk", "k"), (wv, "wv", "v")):
                ktk = f"qtk{pr}"
                bb, bbk = biases[bias]
                for dg in range(2):
                    ps, kps = qkps[dg], f"qkps{dg}"
                    prods = [(xnf[:, 128 * c:128 * (c + 1)],
                              wsb[:, 1024 * c + 512 * dg:1024 * c + 512 * (dg + 1)],
                              [("xnf",), wkey]) for c in range(8)]
                    p.mm_group(ps[:], prods, kps, tag=f"{bias}{j}")
                    p.tt(qtk[pr][:, 512 * dg:512 * (dg + 1)], ps[:],
                         bb[:, 512 * dg:512 * (dg + 1)], ALU.add,
                         r=[kps, bbk], w=[ktk])
                if bias == "v":
                    p.copy("dve", v_tok[:, 1024 * j:1024 * (j + 1)], qtk[pr][:],
                           r=[ktk], w=[("vtok", j)])
                else:
                    dstb = q_fm if bias == "q" else k_fm
                    dkey = ("qfm", j) if bias == "q" else ("kfm", j)
                    transpose_tile_to(p, ident, pool, pr, qtk[pr], ktk,
                                      fap(dstb, 128 * j, [[T, 8], [1, 128]]), dkey)
        p.pop_scope()
        attn_fm = p.sb("afm", [128, 8 * T], BF16)

        p.push_scope()

        def q_ap(h, t128):
            b = 64 * (h % 2)
            return (q_fm[b:b + 64, 2048 * (h // 2) + 128 * t128:
                         2048 * (h // 2) + 128 * (t128 + 1)], [("qfm", t128)])

        def k_ap(h, kc):
            b = 64 * (h % 2)
            return (k_fm[b:b + 64, 2048 * (h // 2) + 512 * kc:
                         2048 * (h // 2) + 512 * (kc + 1)],
                    [("kfm", 4 * kc + i) for i in range(4)])

        def v_ap(h, kbt):
            return (v_tok[:, 1024 * kbt + 64 * h:1024 * kbt + 64 * (h + 1)],
                    [("vtok", kbt)])

        def attn_write(h, g, pv_ap, kpv, QGW=256):
            b = 64 * (h % 2)
            dst = attn_fm[b:b + 64, 2048 * (h // 2) + QGW * g:
                          2048 * (h // 2) + QGW * (g + 1)]
            p.acopy(dst, pv_ap, r=[kpv], w=[("afm", h, g)])

        emit_attn_core(p, f"{uname}ac{s}_", ident, kind="softmax", suffix=True, QGT=2,
                       nheads=16, q_ap=q_ap, k_ap=k_ap, v_ap=v_ap,
                       out_base=lambda h: 64 * (h % 2), attn_write=attn_write,
                       scale=(scl[:, 0:1], "ascl"), maskv=maskv)
        p.pop_scope()

        p.push_scope()
        wo = load_w_fm(p, "pool", w["WoT"], 1024, "wo")
        bob = p.sb("bob", [128, 1024], F32)
        p.dma("sp", bob[:], w["bo_brd"], w=["bob"], tag="ld_bo")

        def afm_rkeys(c, j):
            g = j // 2
            return [("afm", 2 * c, g), ("afm", 2 * c + 1, g)]

        def epilogue(j, dg, ps_ap, kps, xt, kx, dst, ko):
            p.tt(dst, ps_ap, xt[:, 512 * dg:512 * (dg + 1)], ALU.add,
                 r=[kps, kx], w=[ko])
            p.tt(dst, dst, bob[:, 512 * dg:512 * (dg + 1)], ALU.add,
                 r=[ko, "bob"], w=[ko])

        emit_oproj(p, f"{uname}op{s}_", ident, attn_fm, afm_rkeys, wo, "wo",
                   x_in, x_out, s0, epilogue, QGT=2)
        p.pop_scope()
        p.pop_scope()


def l0_attn_a_host_weights(inputs):
    Wq, Wk, Wv, Wo = (inputs["L0_attn_a_Wq"], inputs["L0_attn_a_Wk"],
                      inputs["L0_attn_a_Wv"], inputs["L0_attn_a_Wo"])
    bq, bk, bv, bo = (inputs["L0_attn_a_bq"], inputs["L0_attn_a_bk"],
                      inputs["L0_attn_a_bv"], inputs["L0_attn_a_bo"])
    diag = np.zeros((128, 128), np.float32)
    qq = np.arange(128)[:, None]
    kk = np.arange(128)[None, :]
    same64 = (qq // 64) == (kk // 64)
    masked = (same64 & (kk > qq)) | ((qq >= 64) & (kk < 64))
    diag[masked] = NEG
    maskv = np.zeros((4 * 128, 512), np.float32)
    for v in range(4):
        maskv[128 * v:128 * (v + 1), 0:128 * v] = NEG
        maskv[128 * v:128 * (v + 1), 128 * v:128 * (v + 1)] = diag
    return {
        "WqT": np.ascontiguousarray(Wq.T), "WkT": np.ascontiguousarray(Wk.T),
        "WvT": np.ascontiguousarray(Wv.T), "WoT": np.ascontiguousarray(Wo.T),
        "bq_brd": np.ascontiguousarray(np.broadcast_to(bq, (128, 1024))),
        "bk_brd": np.ascontiguousarray(np.broadcast_to(bk, (128, 1024))),
        "bv_brd": np.ascontiguousarray(np.broadcast_to(bv, (128, 1024))),
        "bo_brd": np.ascontiguousarray(np.broadcast_to(bo, (128, 1024))),
        "attn_scale_rep": np.ascontiguousarray(
            np.broadcast_to(inputs["L0_attn_a_attn_scale"], (128, 1))),
        "maskv": maskv,
    }


WSPECS_L0_ATTN_A = [("WqT", [1024, 1024]), ("WkT", [1024, 1024]),
                    ("WvT", [1024, 1024]), ("WoT", [1024, 1024]),
                    ("bq_brd", [128, 1024]), ("bk_brd", [128, 1024]),
                    ("bv_brd", [128, 1024]), ("bo_brd", [128, 1024]),
                    ("attn_scale_rep", [128, 1]), ("maskv", [512, 512])]


def build_l0_attn_b(p, ident, w, x_in, x_out, uname=""):
    maskv = []
    for v in range(4):
        m = p.sb("bmask", [128, 512], F32)
        p.dma("sp", m[:], w["maskv01"][v * 128:(v + 1) * 128, :], w=[("bmask", v)],
              tag="ld_mask")
        maskv.append((m[:], ("bmask", v)))

    for s in range(SEQ_PER_CORE):
        p.push_scope()
        q_fm = p.sb("qfm", [128, 8 * T], BF16)
        k_fm = p.sb("kfm", [128, 8 * T], BF16)
        v_tok = p.sb("vtok", [128, TPS * 1024], BF16)
        s0 = s * TPS

        p.push_scope()
        pool = LnTilePool(p, f"{uname}qv{s}_")
        xnf = p.sb("xnf", [128, 8 * 128], BF16)
        qkps = [p.psum("qkps", [128, 512], F32) for _ in range(2)]
        qtk = [p.sb("qtk", [128, 1024], BF16) for _ in range(2)]
        rtmp = [p.sb("rtmp", [128, 512], F32) for _ in range(2)]
        rtmp2 = [p.sb("rtmp2", [128, 512], F32) for _ in range(2)]
        cosb = [p.sb("cosb", [128, 1024], F32) for _ in range(2)]
        sinb = [p.sb("sinb", [128, 1024], F32) for _ in range(2)]
        wq = load_w_fm(p, "pool", w["wq"], 1024, "wq")
        wk = load_w_fm(p, "pool", w["wk"], 1024, "wk")
        wv = load_w_fm(p, "pool", w["wv"], 1024, "wv")
        for j in range(TPS):
            pr = j % 2
            kcos, ksin = f"cosb{pr}", f"sinb{pr}"
            p.dma("sp", cosb[pr][:], w["COSI"][128 * j:128 * (j + 1), :],
                  w=[kcos], tag="ld_cos")
            p.dma("sp", sinb[pr][:], w["SI2"][128 * j:128 * (j + 1), :],
                  w=[ksin], tag="ld_sin")
            emit_ln_tile(p, ident, pool, j,
                         x_in[128 * (s0 + j):128 * (s0 + j + 1), :],
                         xnf.ap(), ("xnf",))
            for wsb, wkey, kindq in ((wq, "wq", "q"), (wk, "wk", "k"), (wv, "wv", "v")):
                ktk = f"qtk{pr}"
                for dg in range(2):
                    ps, kps = qkps[dg], f"qkps{dg}"
                    prods = [(xnf[:, 128 * c:128 * (c + 1)],
                              wsb[:, 1024 * c + 512 * dg:1024 * c + 512 * (dg + 1)],
                              [("xnf",), wkey]) for c in range(8)]
                    p.mm_group(ps[:], prods, kps, tag=f"{kindq}{j}")
                    dst = qtk[pr][:, 512 * dg:512 * (dg + 1)]
                    if kindq == "v":
                        p.copy("dve", dst, ps[:], r=[kps], w=[ktk])
                    else:
                        kr1, kr2 = f"rtmp{dg}", f"rtmp2{dg}"
                        p.tt(rtmp[dg][:], ps[:], cosb[pr][:, 512 * dg:512 * (dg + 1)],
                             ALU.mult, r=[kps, kcos], w=[kr1])
                        swap_ap = AP(ps.ap().tensor, 1,
                                     [list(ps.ap().ap[0]), [2, 256], [-1, 2]])
                        p.tt(rtmp2[dg][:], swap_ap, sinb[pr][:, 512 * dg:512 * (dg + 1)],
                             ALU.mult, r=[kps, ksin], w=[kr2])
                        p.tt(rtmp[dg][:], rtmp[dg][:], rtmp2[dg][:], ALU.add,
                             r=[kr1, kr2], w=[kr1])
                        p.op("dve", lambda e, o=dst, i=rtmp[dg]: e.tensor_relu(
                             out=o, in_=i[:]), r=[kr1], w=[ktk])
                if kindq == "v":
                    p.copy("dve", v_tok[:, 1024 * j:1024 * (j + 1)], qtk[pr][:],
                           r=[ktk], w=[("vtok", j)])
                else:
                    dstb = q_fm if kindq == "q" else k_fm
                    dkey = ("qfm", j) if kindq == "q" else ("kfm", j)
                    transpose_tile_to(p, ident, pool, pr, qtk[pr], ktk,
                                      fap(dstb, 128 * j, [[T, 8], [1, 128]]), dkey)
        p.pop_scope()
        attn_fm = p.sb("afm", [128, 8 * T], BF16)

        p.push_scope()

        def q_ap(h, t128):
            b = 64 * (h % 2)
            return (q_fm[b:b + 64, 2048 * (h // 2) + 128 * t128:
                         2048 * (h // 2) + 128 * (t128 + 1)], [("qfm", t128)])

        def k_ap(h, kc):
            b = 64 * (h % 2)
            return (k_fm[b:b + 64, 2048 * (h // 2) + 512 * kc:
                         2048 * (h // 2) + 512 * (kc + 1)],
                    [("kfm", 4 * kc + i) for i in range(4)])

        def v_ap(h, kbt):
            return (v_tok[:, 1024 * kbt + 64 * h:1024 * kbt + 64 * (h + 1)],
                    [("vtok", kbt)])

        def attn_write(h, g, pv_ap, kpv, QGW=256):
            b = 64 * (h % 2)
            dst = attn_fm[b:b + 64, 2048 * (h // 2) + QGW * g:
                          2048 * (h // 2) + QGW * (g + 1)]
            p.acopy(dst, pv_ap, r=[kpv], w=[("afm", h, g)])

        emit_attn_core(p, f"{uname}bc{s}_", ident, kind="linear", suffix=False, QGT=2,
                       nheads=16, q_ap=q_ap, k_ap=k_ap, v_ap=v_ap,
                       out_base=lambda h: 64 * (h % 2), attn_write=attn_write,
                       scale=None, maskv=maskv)
        p.pop_scope()

        p.push_scope()
        wo = load_w_fm(p, "pool", w["wo"], 1024, "wo")

        def afm_rkeys(c, j):
            g = j // 2
            return [("afm", 2 * c, g), ("afm", 2 * c + 1, g)]

        def epilogue(j, dg, ps_ap, kps, xt, kx, dst, ko):
            p.tt(dst, ps_ap, xt[:, 512 * dg:512 * (dg + 1)], ALU.add,
                 r=[kps, kx], w=[ko])

        emit_oproj(p, f"{uname}op{s}_", ident, attn_fm, afm_rkeys, wo, "wo",
                   x_in, x_out, s0, epilogue, QGT=2)
        p.pop_scope()
        p.pop_scope()


def l0_attn_b_host_weights(inputs):
    d = 64
    inv_freq = 1.0 / (10000.0 ** (np.arange(0, d, 2, dtype=np.float32) / d))
    t = np.arange(T, dtype=np.float32)
    freqs = (t[:, None] * inv_freq[None, :]).astype(np.float32)
    cos = np.cos(freqs.astype(np.float64)).astype(np.float32)
    sin = np.sin(freqs.astype(np.float64)).astype(np.float32)
    cosi = np.repeat(cos, 2, axis=1)
    si2 = np.empty((T, 64), np.float32)
    si2[:, 0::2] = -sin
    si2[:, 1::2] = sin
    COSI = np.tile(cosi, (1, 16))
    SI2 = np.tile(si2, (1, 16))
    tril = np.tril(np.ones((128, 128), np.float32))
    maskv = np.zeros((4 * 128, 512), np.float32)
    for v in range(4):
        maskv[128 * v:128 * (v + 1), 0:128 * v] = 1.0
        maskv[128 * v:128 * (v + 1), 128 * v:128 * (v + 1)] = tril
    return {
        "wq": np.ascontiguousarray(inputs["L0_attn_b_wq"]),
        "wk": np.ascontiguousarray(inputs["L0_attn_b_wk"]),
        "wv": np.ascontiguousarray(inputs["L0_attn_b_wv"]),
        "wo": np.ascontiguousarray(inputs["L0_attn_b_wo"]),
        "COSI": COSI, "SI2": SI2, "maskv01": maskv,
    }


WSPECS_L0_ATTN_B = [("wq", [1024, 1024]), ("wk", [1024, 1024]),
                    ("wv", [1024, 1024]), ("wo", [1024, 1024]),
                    ("COSI", [2048, 1024]), ("SI2", [2048, 1024]),
                    ("maskv01", [512, 512])]


PERM_A = [0, 1, 4, 5, 8, 9, 12, 13, 2, 3, 6, 7, 10, 11, 14, 15]


def build_l1_attn_a(p, ident, w, x_in, x_out, uname=""):
    maskv = []
    for v in range(4):
        m = p.sb("cmask", [128, 512], F32)
        p.dma("sp", m[:], w["maskv"][v * 128:(v + 1) * 128, :], w=[("cmask", v)],
              tag="ld_mask")
        maskv.append((m[:], ("cmask", v)))
    nwb = p.sb("nwb", [128, 1024], F32)
    p.dma("sp", nwb[:], w["normw_brd"], w=["nwb"], tag="ld_nw")

    for s in range(SEQ_PER_CORE):
        p.push_scope()
        q_fm = p.sb("qfm", [128, 8 * T], BF16)
        k_fm = p.sb("kfm", [128, 2 * T], BF16)
        v_tok = p.sb("vtok", [128, TPS * 256], BF16)
        s0 = s * TPS

        p.push_scope()
        pool = LnTilePool(p, f"{uname}qa{s}_", rms=True)
        xnf = p.sb("xnf", [128, 8 * 128], BF16)
        qkps = [p.psum("qkps", [128, 512], F32) for _ in range(2)]
        qtk = [p.sb("qtk", [128, 1024], BF16) for _ in range(2)]
        rfb = [p.sb("rfb", [128, 1024], F32) for _ in range(2)]
        wqp = load_w_fm(p, "pool", w["WqP"], 1024, "wqp")
        wkk = load_w_fm(p, "pool", w["w_k"], 256, "wkk")
        wvv = load_w_fm(p, "pool", w["w_v"], 256, "wvv")
        for j in range(TPS):
            pr = j % 2
            krf = f"rfb{pr}"
            p.dma("sp", rfb[pr][:], w["RF_rep"][128 * j:128 * (j + 1), :],
                  w=[krf], tag="ld_rf")
            emit_ln_tile(p, ident, pool, j,
                         x_in[128 * (s0 + j):128 * (s0 + j + 1), :],
                         xnf.ap(), ("xnf",), normw=(nwb, "nwb"))
            ktk = f"qtk{pr}"
            for dg in range(2):
                ps, kps = qkps[dg], f"qkps{dg}"
                prods = [(xnf[:, 128 * c:128 * (c + 1)],
                          wqp[:, 1024 * c + 512 * dg:1024 * c + 512 * (dg + 1)],
                          [("xnf",), "wqp"]) for c in range(8)]
                p.mm_group(ps[:], prods, kps, tag=f"q{j}")
                p.tt(qtk[pr][:, 512 * dg:512 * (dg + 1)], ps[:],
                     rfb[pr][:, 512 * dg:512 * (dg + 1)], ALU.mult,
                     r=[kps, krf], w=[ktk])
            transpose_tile_to(p, ident, pool, pr, qtk[pr], ktk,
                              fap(q_fm, 128 * j, [[T, 8], [1, 128]]), ("qfm", j))
            for wsb, wkey, kindq in ((wkk, "wkk", "k"), (wvv, "wvv", "v")):
                ps, kps = qkps[0], "qkps0"
                prods = [(xnf[:, 128 * c:128 * (c + 1)],
                          wsb[:, 256 * c:256 * (c + 1)],
                          [("xnf",), wkey]) for c in range(8)]
                p.mm_group(ps[:, 0:256], prods, kps, tag=f"{kindq}{j}")
                if kindq == "k":
                    p.tt(qtk[pr][:, 0:256], ps[:, 0:256], rfb[pr][:, 0:256],
                         ALU.mult, r=[kps, krf], w=[ktk])
                    transpose_tile_to(p, ident, pool, pr, qtk[pr], ktk,
                                      fap(k_fm, 128 * j, [[T, 2], [1, 128]]),
                                      ("kfm", j), nch=2)
                else:
                    p.copy("dve", v_tok[:, 256 * j:256 * (j + 1)], ps[:, 0:256],
                           r=[kps], w=[("vtok", j)])
        p.pop_scope()
        attn_fm = p.sb("afm", [128, 8 * T], BF16)

        p.push_scope()

        def kv_of(pos):
            return (pos % 2) + 2 * (pos // 8)

        def q_ap(h, t128):
            b = 64 * (h % 2)
            return (q_fm[b:b + 64, 2048 * (h // 2) + 128 * t128:
                         2048 * (h // 2) + 128 * (t128 + 1)], [("qfm", t128)])

        def k_ap(h, kc):
            c = kv_of(h)
            b = 64 * (c % 2)
            return (k_fm[b:b + 64, 2048 * (c // 2) + 512 * kc:
                         2048 * (c // 2) + 512 * (kc + 1)],
                    [("kfm", 4 * kc + i) for i in range(4)])

        def v_ap(h, kbt):
            c = kv_of(h)
            return (v_tok[:, 256 * kbt + 64 * c:256 * kbt + 64 * (c + 1)],
                    [("vtok", kbt)])

        def attn_write(h, g, pv_ap, kpv, QGW=256):
            b = 64 * (h % 2)
            dst = attn_fm[b:b + 64, 2048 * (h // 2) + QGW * g:
                          2048 * (h // 2) + QGW * (g + 1)]
            p.acopy(dst, pv_ap, r=[kpv], w=[("afm", h, g)])

        emit_attn_core(p, f"{uname}cc{s}_", ident, kind="softmax", suffix=False, QGT=2,
                       nheads=16, q_ap=q_ap, k_ap=k_ap, v_ap=v_ap,
                       out_base=lambda h: 64 * (h % 2), attn_write=attn_write,
                       scale=0.125, maskv=maskv)
        p.pop_scope()

        p.push_scope()
        wop = load_w_fm(p, "pool", w["WoP"], 1024, "wop")
        lst6 = [p.sb("lst6", [128, 12], F32) for _ in range(2)]
        lmv = [p.sb("lmv", [128, 2], F32) for _ in range(2)]
        lsd = [p.sb("lsd", [128, 1], F32) for _ in range(2)]
        xlnb = [p.sb("xlnb", [128, 1024], F32) for _ in range(2)]

        def afm_rkeys(c, j):
            g = j // 2
            return [("afm", 2 * c, g), ("afm", 2 * c + 1, g)]

        def epilogue(j, dg, ps_ap, kps, xt, kx, dst, ko):
            pr = j % 2
            ks6, kmv, ksd, kxl = (f"l{n}{pr}" for n in ("st6", "mv", "sd", "xlnb"))
            if dg == 0:
                p.op("dve", lambda e, a=lst6[pr], b=xt: e.bn_stats(
                     out=a[:, 0:6], in_=b[:, 0:512]), r=[kx], w=[ks6])
                p.op("dve", lambda e, a=lst6[pr], b=xt: e.bn_stats(
                     out=a[:, 6:12], in_=b[:, 512:1024]), r=[kx], w=[ks6])
                p.op("dve", lambda e, a=lmv[pr], b=lst6[pr]: e.bn_aggr(
                     out=a[:, 0:2], in_=b[:]), r=[ks6], w=[kmv])
                eps_ap, eps_key = p.constf(1e-5)
                p.act(lsd[pr][:], lmv[pr][:, 1:2], AF.Sqrt, bias=eps_ap,
                      r=[kmv, eps_key], w=[ksd])
                p.recip(lsd[pr][:], lsd[pr][:], r=[ksd], w=[ksd])
                p.ts(xlnb[pr][:], xt[:], lmv[pr][:, 0:1], lsd[pr][:, 0:1],
                     ALU.subtract, ALU.mult, r=[kx, kmv, ksd], w=[kxl])
            p.tt(dst, ps_ap, xt[:, 512 * dg:512 * (dg + 1)], ALU.add,
                 r=[kps, kx], w=[ko])
            p.tt(dst, dst, xlnb[pr][:, 512 * dg:512 * (dg + 1)], ALU.add,
                 r=[ko, kxl], w=[ko])

        emit_oproj(p, f"{uname}op{s}_", ident, attn_fm, afm_rkeys, wop, "wop",
                   x_in, x_out, s0, epilogue, QGT=4)
        p.pop_scope()
        p.pop_scope()


def l1_attn_a_host_weights(inputs):
    w_q, w_k, w_v, w_o = (inputs["L1_attn_a_w_q"], inputs["L1_attn_a_w_k"],
                          inputs["L1_attn_a_w_v"], inputs["L1_attn_a_w_o"])
    norm_w = inputs["L1_attn_a_norm_w"]
    WqP = np.concatenate([w_q[:, 64 * h:64 * (h + 1)] for h in PERM_A], axis=1)
    WoP = np.concatenate([w_o[64 * h:64 * (h + 1), :] for h in PERM_A], axis=0)
    d_head = 64
    ii = np.arange(d_head)
    theta = (10000.0 ** (-2.0 * (ii // 2).astype(np.float64) / d_head))
    pos = np.arange(T, dtype=np.float64)
    ang = pos[:, None] * theta[None, :]
    sgn = np.where(np.arange(T) % 2 == 0, -1.0, 1.0)[:, None]
    rf = (np.cos(ang) + np.sin(ang) * sgn).astype(np.float32)
    RF_rep = np.tile(rf, (1, 16))
    qq = np.arange(128)[:, None]
    kk = np.arange(128)[None, :]
    diag = np.where(kk > qq, NEG, 0.0).astype(np.float32)
    maskv = np.zeros((4 * 128, 512), np.float32)
    for v in range(4):
        maskv[128 * v:128 * (v + 1), 128 * v:128 * (v + 1)] = diag
        maskv[128 * v:128 * (v + 1), 128 * (v + 1):] = NEG
    return {
        "WqP": np.ascontiguousarray(WqP),
        "w_k": np.ascontiguousarray(w_k), "w_v": np.ascontiguousarray(w_v),
        "WoP": np.ascontiguousarray(WoP),
        "normw_brd": np.ascontiguousarray(np.broadcast_to(norm_w, (128, 1024))),
        "RF_rep": RF_rep, "maskv": maskv,
    }


WSPECS_L1_ATTN_A = [("WqP", [1024, 1024]), ("w_k", [1024, 256]),
                    ("w_v", [1024, 256]), ("WoP", [1024, 1024]),
                    ("normw_brd", [128, 1024]), ("RF_rep", [2048, 1024]),
                    ("maskv", [512, 512])]


def build_l1_attn_b(p, ident, w, x_in, x_out, uname=""):
    ones_bf = p.sb("ones1", [128, 1], BF16)
    p.op("pool", lambda e, a=ones_bf: e.memset(a[:], 1.0), w=["ones1"])

    for s in range(SEQ_PER_CORE):
        p.push_scope()
        qf_fm = p.sb("qffm", [128, 8 * T], BF16)
        kf_tok = p.sb("kftok", [128, TPS * 1024], BF16)
        v_tok = p.sb("vtok", [128, TPS * 1024], BF16)
        s0 = s * TPS

        p.push_scope()
        pool = LnTilePool(p, f"{uname}qb{s}_")
        xnf = p.sb("xnf", [128, 8 * 128], BF16)
        qkps = [p.psum("qkps", [128, 512], F32) for _ in range(2)]
        qtk = [p.sb("qtk", [128, 1024], BF16) for _ in range(2)]
        rtmp = [p.sb("rtmp", [128, 512], F32) for _ in range(3)]
        cosb = [p.sb("cosb", [128, 1024], F32) for _ in range(2)]
        sinb = [p.sb("sinb", [128, 1024], F32) for _ in range(2)]
        wqkv = p.sb("wqkv", [128, 8 * 3072], BF16)
        p.dma("pool", wqkv[:], w["W_qkv"].rearrange("(c p) d -> p c d", p=128),
              w=["wqkv"], tag="ld_wqkv")
        for j in range(TPS):
            pr = j % 2
            kcos, ksin = f"cosb{pr}", f"sinb{pr}"
            p.dma("sp", cosb[pr][:], w["COS2"][128 * j:128 * (j + 1), :],
                  w=[kcos], tag="ld_cos")
            p.dma("sp", sinb[pr][:], w["SIN2"][128 * j:128 * (j + 1), :],
                  w=[ksin], tag="ld_sin")
            emit_ln_tile(p, ident, pool, j,
                         x_in[128 * (s0 + j):128 * (s0 + j + 1), :],
                         xnf.ap(), ("xnf",))
            for part in range(6):
                ps, kps = qkps[part % 2], f"qkps{part % 2}"
                prods = [(xnf[:, 128 * c:128 * (c + 1)],
                          wqkv[:, 3072 * c + 512 * part:3072 * c + 512 * (part + 1)],
                          [("xnf",), "wqkv"]) for c in range(8)]
                p.mm_group(ps[:], prods, kps, tag=f"qkv{j}")
                kindq = "qkv"[part // 2]
                dgl = part % 2
                ktk = f"qtk{pr}"
                if kindq == "v":
                    p.copy("dve", v_tok[:, 1024 * j + 512 * dgl:
                                        1024 * j + 512 * (dgl + 1)], ps[:],
                           r=[kps], w=[("vtok", j)])
                    continue
                kr0, kr1, kr2 = "rt0", "rt1", "rt2"
                p.tt(rtmp[0][:], ps[:], cosb[pr][:, 512 * dgl:512 * (dgl + 1)],
                     ALU.mult, r=[kps, kcos], w=[kr0])
                hs_ap = AP(ps.ap().tensor, 32,
                           [list(ps.ap().ap[0]), [64, 8], [-32, 2], [1, 32]])
                p.tt(rtmp[1][:], hs_ap, sinb[pr][:, 512 * dgl:512 * (dgl + 1)],
                     ALU.mult, r=[kps, ksin], w=[kr1])
                p.tt(rtmp[0][:], rtmp[0][:], rtmp[1][:], ALU.add,
                     r=[kr0, kr1], w=[kr0])
                sc = 0.125 if kindq == "q" else 1.0
                p.ts(rtmp[1][:], rtmp[0][:], sc, 0.0, ALU.mult, ALU.min,
                     r=[kr0], w=[kr1])
                p.op("act", lambda e, o=rtmp[2], i=rtmp[1]: e.activation(
                     out=o[:], in_=i[:], func=AF.Exp), r=[kr1], w=[kr2])
                p.ts(rtmp[1][:], rtmp[0][:], sc, 0.0, ALU.mult, ALU.max,
                     r=[kr0], w=[kr1])
                p.tt(qtk[pr][:, 512 * dgl:512 * (dgl + 1)], rtmp[1][:],
                     rtmp[2][:], ALU.add, r=[kr1, kr2], w=[ktk])
                if kindq == "q" and dgl == 1:
                    transpose_tile_to(p, ident, pool, pr, qtk[pr], ktk,
                                      fap(qf_fm, 128 * j, [[T, 8], [1, 128]]),
                                      ("qffm", j))
                elif kindq == "k" and dgl == 1:
                    p.copy("dve", kf_tok[:, 1024 * j:1024 * (j + 1)], qtk[pr][:],
                           r=[ktk], w=[("kftok", j)])
        p.pop_scope()
        attn_fm = p.sb("afm2", [128, 8 * T], BF16)

        p.push_scope()
        kvps = [p.psum("kvps", [128, 128], F32) for _ in range(2)]
        kvh = p.sb("kvh", [128, 8 * 65], BF16)
        attn_tok = [p.sb("atok", [128, 1024], BF16) for _ in(0, 1)]
        for h in range(16):
            b = 64 * (h % 2)
            pi = h % 2
            ps, kps = kvps[pi], f"kvps{pi}"
            tp = (0, 64) if b else None
            for kbt in range(TPS):
                p.mm(ps[b:b + 64, 0:64],
                     kf_tok[:, 1024 * kbt + 64 * h:1024 * kbt + 64 * (h + 1)],
                     v_tok[:, 1024 * kbt + 64 * h:1024 * kbt + 64 * (h + 1)],
                     kbt == 0, kbt == TPS - 1,
                     r=[("kftok", kbt), ("vtok", kbt)],
                     w=([kps] if kbt == 0 else ()), a=([kps] if kbt else ()),
                     tag="kv", tile_position=tp)
            for kbt in range(TPS):
                p.mm(ps[b:b + 64, 64:65],
                     kf_tok[:, 1024 * kbt + 64 * h:1024 * kbt + 64 * (h + 1)],
                     ones_bf[:], kbt == 0, kbt == TPS - 1,
                     r=[("kftok", kbt), "ones1"], a=[kps],
                     tag="ksum", tile_position=tp)
            p.acopy(kvh[b:b + 64, 65 * (h // 2):65 * (h // 2) + 65],
                    ps[b:b + 64, 0:65], r=[kps], w=[("kvh", h)])
        ops2 = [p.psum("obps", [128, 128], F32) for _ in (0, 1)]
        dsc = [p.sb("dsc", [128, 1], F32) for _ in (0, 1)]
        tp_pool = PtPool(p, f"{uname}p3{s}_")
        for j in range(TPS):
            pr = j % 2
            katk = f"atok{pr}"
            for h in range(16):
                b = 64 * (h % 2)
                pi = h % 2
                ps, kps = ops2[pi], f"obps{pi}"
                tp = (64, 0) if b else None
                p.mm(ps[:, 0:65],
                     qf_fm[b:b + 64, 2048 * (h // 2) + 128 * j:
                           2048 * (h // 2) + 128 * (j + 1)],
                     kvh[b:b + 64, 65 * (h // 2):65 * (h // 2) + 65],
                     True, True, r=[("qffm", j), ("kvh", h)], w=[kps],
                     tag="ob", tile_position=tp)
                kd = f"dsc{pi}"
                p.op("dve", lambda e, o=dsc[pi], i=ps: e.tensor_scalar_max(
                     out=o[:], in0=i[:, 64:65], scalar1=1e-6), r=[kps], w=[kd])
                p.recip(dsc[pi][:], dsc[pi][:], r=[kd], w=[kd])
                p.acopy(attn_tok[pr][:, 64 * h:64 * (h + 1)], ps[:, 0:64],
                        scale=dsc[pi][:, 0:1], r=[kps, kd], w=[katk])
            transpose_tile_to(p, ident, tp_pool, pr, attn_tok[pr], katk,
                              fap(attn_fm, 128 * j, [[T, 8], [1, 128]]),
                              ("afm2", j))
        p.pop_scope()

        p.push_scope()
        wo = load_w_fm(p, "pool", w["W_out"], 1024, "wo2")

        def afm_rkeys(c, j):
            return [("afm2", j)]

        def epilogue(j, dg, ps_ap, kps, xt, kx, dst, ko):
            p.tt(dst, ps_ap, xt[:, 512 * dg:512 * (dg + 1)], ALU.add,
                 r=[kps, kx], w=[ko])

        emit_oproj(p, f"{uname}op{s}_", ident, attn_fm, afm_rkeys, wo, "wo2",
                   x_in, x_out, s0, epilogue, QGT=4)
        p.pop_scope()
        p.pop_scope()


def l1_attn_b_host_weights(inputs):
    hd = 64
    inv_freq = 1.0 / (500000.0 ** (np.arange(0, hd, 2, dtype=np.float32) / hd))
    t = np.arange(T, dtype=np.float32)
    freqs = (t[:, None] * inv_freq[None, :]).astype(np.float32)
    emb = np.concatenate([freqs, freqs], axis=1)
    cos = np.cos(emb.astype(np.float64)).astype(np.float32)
    sin = np.sin(emb.astype(np.float64)).astype(np.float32)
    sin_sgn = sin.copy()
    sin_sgn[:, 0:32] = -sin[:, 0:32]
    COS2 = np.tile(cos, (1, 16))
    SIN2 = np.tile(sin_sgn, (1, 16))
    return {
        "W_qkv": np.ascontiguousarray(inputs["L1_attn_b_W_qkv"]),
        "W_out": np.ascontiguousarray(inputs["L1_attn_b_W_out"]),
        "COS2": COS2, "SIN2": SIN2,
    }


WSPECS_L1_ATTN_B = [("W_qkv", [1024, 3072]), ("W_out", [1024, 1024]),
                    ("COS2", [2048, 1024]), ("SIN2", [2048, 1024])]


def build_l1_ffn(p, ident, w, x_in, x_out, uname=""):
    GT = 1024
    GTILES = GT // 128
    NG = NTOK // GT

    p.push_scope()
    gw = p.sb("gw", [128, 64], F32)
    p.dma("sp", gw[:], w["gate_wT"].rearrange("(c p) a -> p c a", p=128),
          w=["gw"], tag="ld_gw")
    gwh = p.sb("gwh", [128, 64], BF16)
    gwl = p.sb("gwl", [128, 64], BF16)
    p.copy("dve", gwh[:], gw[:], r=["gw"], w=["gwh"])
    p.tt(gwl[:], gw[:], gwh[:], ALU.subtract, r=["gw", "gwh"], w=["gwl"])

    p.push_scope()
    wsl = [p.sb("wsl", [128, 11264], BF16) for _ in range(2)]
    for g in range(NG):
        p.push_scope()
        xn = p.sb("xn", [128, 8 * GT], BF16)
        comb = p.sb("comb", [128, 8 * GTILES], F32)
        acc = p.sb("acc", [128, 1024 * GTILES], F32)

        p.push_scope()
        xn_lo = p.sb("xnlo", [128, 8 * GT], BF16)
        zps = p.psum("zps", [128, 64], F32)
        z_sb = p.sb("z", [128, 8], F32)
        rsc = router_scratch(p, 8)

        def hook(j, gi, g=g):
            tg = f"{uname}g{g}t{j}"
            prods = []
            for c in range(8):
                xn_c = xn[:, fm_cols(GT, c, 128 * j, 128)]
                lo_c = xn_lo[:, fm_cols(GT, c, 128 * j, 128)]
                gs = slice(8 * c, 8 * (c + 1))
                prods.append((xn_c, gwh[:, gs], [("xnfm", j), "gwh"]))
                prods.append((xn_c, gwl[:, gs], [("xnfm", j), "gwl"]))
                prods.append((lo_c, gwh[:, gs], [("xnlo", j), "gwh"]))
            p.mm_group(zps[:, 0:8], prods, "zps", tag=f"lg{tg}")
            p.copy("dve", z_sb[:], zps[:, 0:8], r=["zps"], w=["zsb"])
            emit_router(p, z_sb, "zsb", 0, 8, 2, comb[:, 8 * j:8 * (j + 1)],
                        ("comb", j), 2.5, rsc, tag=f"{tg}r")

        emit_ln_to_fm(p, ident, lambda gi: x_in[128 * gi:128 * (gi + 1), :],
                      [(j, g * GTILES + j) for j in range(GTILES)],
                      xn, "xnfm", GT, lo_fm=xn_lo, lo_key="xnlo",
                      tag=f"{uname}l1f{g}_", per_tile_hook=hook)
        p.pop_scope()

        p.push_scope()
        g_fm = p.sb("gfm", [128, 11 * GT], BF16)
        eps_ = [p.psum("eps", [128, 512], F32) for _ in range(2)]
        silu_t = [p.sb("silt", [128, 512], F32) for _ in range(2)]
        slot_cnt = [0]

        def wload(name, e, shp):
            sl = slot_cnt[0] % 2
            slot_cnt[0] += 1
            kw = f"ewsl{sl}"
            if name in ("w_gate", "w_up"):
                src = w[name][e].rearrange("(c p) f -> p c f", p=128)
            else:
                src = w[name][e].rearrange("(hc p) d -> p hc d", p=128)
            p.dma("pool", wsl[sl][:], src, w=[kw], tag=f"ld{name}{e}",
                  nofence=(slot_cnt[0] > 2))
            return wsl[sl], kw

        for e in range(8):
            wgs, kwg = wload("w_gate", e, None)
            wus, kwu = wload("w_up", e, None)
            for hc in range(11):
                for tg in range(GT // 512):
                    gps, kgps = eps_[0], "eps0"
                    ups, kups = eps_[1], "eps1"
                    gp = [(wgs[:, 1408 * c + 128 * hc:1408 * c + 128 * (hc + 1)],
                           xn[:, fm_cols(GT, c, 512 * tg, 512)],
                           [kwg] + [("xnfm", 4 * tg + i) for i in range(4)])
                          for c in range(8)]
                    p.mm_group(gps[:], gp, kgps, tag=f"ge{e}")
                    up = [(wus[:, 1408 * c + 128 * hc:1408 * c + 128 * (hc + 1)],
                           xn[:, fm_cols(GT, c, 512 * tg, 512)],
                           [kwu] + [("xnfm", 4 * tg + i) for i in range(4)])
                          for c in range(8)]
                    p.mm_group(ups[:], up, kups, tag=f"ue{e}")
                    ksl = f"silt{tg % 2}"
                    p.op("act", lambda e2, o=silu_t[tg % 2], i=gps: e2.activation(
                         out=o[:], in_=i[:], func=AF.Silu), r=[kgps], w=[ksl])
                    p.tt(g_fm[:, fm_cols(GT, 0, 0, 0).start + GT * hc + 512 * tg:
                              GT * hc + 512 * (tg + 1)],
                         silu_t[tg % 2][:], ups[:], ALU.mult,
                         r=[ksl, kups], w=[("gfm", hc, tg)])
            wds, kwd = wload("w_down", e, None)
            for j in range(GTILES):
                for dg in range(2):
                    pi = (2 * j + dg) % 2
                    ps, kps = eps_[pi], f"eps{pi}"
                    prods = [(g_fm[:, GT * hc + 128 * j:GT * hc + 128 * (j + 1)],
                              wds[:, 1024 * hc + 512 * dg:1024 * hc + 512 * (dg + 1)],
                              [("gfm", hc, j // 4), kwd]) for hc in range(11)]
                    p.mm_group(ps[:], prods, kps, tag=f"de{e}")
                    dst = acc[:, 1024 * j + 512 * dg:1024 * j + 512 * (dg + 1)]
                    kacc = ("acc", j, dg)
                    cs = comb[:, 8 * j + e:8 * j + e + 1]
                    if e == 0:
                        p.ts(dst, ps[:], cs, None, ALU.mult, ALU.bypass,
                             r=[kps, ("comb", j)], w=[kacc])
                    else:
                        p.stt(dst, ps[:], cs, dst, ALU.mult, ALU.add,
                              r=[kps, ("comb", j), kacc], w=[kacc])
        xt2 = [p.sb("xt2", [128, 1024], F32) for _ in range(2)]
        for j in range(GTILES):
            pr = j % 2
            kx = f"xt2{pr}"
            gi = g * GTILES + j
            p.dma("sp", xt2[pr][:], x_in[128 * gi:128 * (gi + 1), :], w=[kx],
                  tag="ldx2")
            for dg in range(2):
                dst = acc[:, 1024 * j + 512 * dg:1024 * j + 512 * (dg + 1)]
                p.tt(dst, dst, xt2[pr][:, 512 * dg:512 * (dg + 1)], ALU.add,
                     r=[("acc", j, dg), kx], w=[("acc", j, dg)])
            p.dma("sp", x_out[128 * gi:128 * (gi + 1), :],
                  acc[:, 1024 * j:1024 * (j + 1)],
                  r=[("acc", j, 0), ("acc", j, 1)], tag="stx")
        p.pop_scope()
        p.pop_scope()
    p.pop_scope()

    p.push_scope()
    swg = p.sb("swg", [128, 8 * 2816], BF16)
    swu = p.sb("swu", [128, 8 * 2816], BF16)
    swd = p.sb("swd", [128, 22 * 1024], BF16)
    p.dma("pool", swg[:], w["sw_gate"].rearrange("(c p) f -> p c f", p=128),
          w=["swg"], tag="ld_swg")
    p.dma("pool", swu[:], w["sw_up"].rearrange("(c p) f -> p c f", p=128),
          w=["swu"], tag="ld_swu")
    p.dma("pool", swd[:], w["sw_down"].rearrange("(hc p) d -> p hc d", p=128),
          w=["swd"], tag="ld_swd")
    pool = LnTilePool(p, f"{uname}sw_")
    xnf2 = p.sb("xnf2", [128, 8 * 256], BF16)
    gbuf = p.sb("gbuf", [128, 22 * 256], BF16)
    gups = [p.psum("gups", [128, 256], F32) for _ in range(2)]
    dps = [p.psum("dps", [128, 512], F32) for _ in range(4)]
    silu_t = [p.sb("silt2", [128, 256], F32)]
    xo = [p.sb("xo", [128, 1024], F32) for _ in range(2)]
    for tp2 in range(NTOK // 256):
        for jj in range(2):
            gi = 2 * tp2 + jj
            emit_ln_tile(p, ident, pool, gi,
                         x_in[128 * gi:128 * (gi + 1), :],
                         fap(xnf2, 128 * jj, [[256, 8], [1, 128]]),
                         ("xnf2", jj))
        for hc in range(22):
            gps, kgps = gups[0], "gups0"
            ups, kups = gups[1], "gups1"
            gp = [(swg[:, 2816 * c + 128 * hc:2816 * c + 128 * (hc + 1)],
                   xnf2[:, 256 * c:256 * (c + 1)],
                   ["swg", ("xnf2", 0), ("xnf2", 1)]) for c in range(8)]
            p.mm_group(gps[:], gp, kgps, tag="swg")
            up = [(swu[:, 2816 * c + 128 * hc:2816 * c + 128 * (hc + 1)],
                   xnf2[:, 256 * c:256 * (c + 1)],
                   ["swu", ("xnf2", 0), ("xnf2", 1)]) for c in range(8)]
            p.mm_group(ups[:], up, kups, tag="swu")
            p.op("act", lambda e2, o=silu_t[0], i=gps: e2.activation(
                 out=o[:], in_=i[:], func=AF.Silu), r=[kgps], w=["silt2"])
            p.tt(gbuf[:, 256 * hc:256 * (hc + 1)], silu_t[0][:], ups[:],
                 ALU.mult, r=["silt2", kups], w=[("gbuf", hc)])
            for jj in range(2):
                for dg in range(2):
                    di = 2 * jj + dg
                    p.mm(dps[di][:], gbuf[:, 256 * hc + 128 * jj:
                                          256 * hc + 128 * (jj + 1)],
                         swd[:, 1024 * hc + 512 * dg:1024 * hc + 512 * (dg + 1)],
                         hc == 0, hc == 21,
                         r=[("gbuf", hc), "swd"],
                         w=([f"dps{di}"] if hc == 0 else ()),
                         a=([f"dps{di}"] if hc > 0 else ()), tag="swd")
        for jj in range(2):
            gi = 2 * tp2 + jj
            pr = jj
            kxo = f"xo{pr}"
            p.dma("sp", xo[pr][:], x_out[128 * gi:128 * (gi + 1), :], w=[kxo],
                  tag="ldxo")
            for dg in range(2):
                di = 2 * jj + dg
                p.tt(xo[pr][:, 512 * dg:512 * (dg + 1)],
                     xo[pr][:, 512 * dg:512 * (dg + 1)], dps[di][:], ALU.add,
                     r=[kxo, f"dps{di}"], w=[kxo])
            p.dma("sp", x_out[128 * gi:128 * (gi + 1), :], xo[pr][:],
                  r=[kxo], tag="stxo")
    p.pop_scope()
    p.pop_scope()


def l1_ffn_host_weights(inputs):
    return {
        "gate_wT": np.ascontiguousarray(inputs["L1_ffn_gate_w"].T),
        "w_gate": np.ascontiguousarray(inputs["L1_ffn_w_gate"]),
        "w_up": np.ascontiguousarray(inputs["L1_ffn_w_up"]),
        "w_down": np.ascontiguousarray(inputs["L1_ffn_w_down"]),
        "sw_gate": np.ascontiguousarray(inputs["L1_ffn_sw_gate"]),
        "sw_up": np.ascontiguousarray(inputs["L1_ffn_sw_up"]),
        "sw_down": np.ascontiguousarray(inputs["L1_ffn_sw_down"]),
    }


WSPECS_L1_FFN = [("gate_wT", [1024, 8]), ("w_gate", [8, 1024, 1408]),
                 ("w_up", [8, 1024, 1408]), ("w_down", [8, 1408, 1024]),
                 ("sw_gate", [1024, 2816]), ("sw_up", [1024, 2816]),
                 ("sw_down", [2816, 1024])]



SUBLAYER_DEFS = [
    (build_l0_attn_a, WSPECS_L0_ATTN_A, l0_attn_a_host_weights),
    (build_l0_ffn, WSPECS_L0_FFN, l0_ffn_host_weights),
    (build_l0_attn_b, WSPECS_L0_ATTN_B, l0_attn_b_host_weights),
    (build_l0_ffn, WSPECS_L0_FFN, l0_ffn_host_weights),
    (build_l1_attn_a, WSPECS_L1_ATTN_A, l1_attn_a_host_weights),
    (build_l1_ffn, WSPECS_L1_FFN, l1_ffn_host_weights),
    (build_l1_attn_b, WSPECS_L1_ATTN_B, l1_attn_b_host_weights),
    (build_l1_ffn, WSPECS_L1_FFN, l1_ffn_host_weights),
]


def build_sublayer_nc(builder, wspecs):
    nc = bass.Bass("TRN2", target_bir_lowering=False)
    x_in = nc.dram_tensor("x_in", [NTOK, D], F32, kind="ExternalInput").ap()
    x_out = nc.dram_tensor("x_out", [NTOK, D], F32, kind="ExternalOutput").ap()
    w = {name: nc.dram_tensor(name, shape, F32, kind="ExternalInput").ap()
         for name, shape in wspecs}
    w["c_ident"] = nc.dram_tensor("c_ident", [128, 128], BF16,
                                  kind="ExternalInput").ap()
    p = Prog(nc)
    ident = load_ident(p, w)
    builder(p, ident, w, x_in, x_out)
    n = p.emit_all()
    return nc, n


_NC_CACHE = {}


def kernel(**inputs):
    x = np.ascontiguousarray(inputs["L0_attn_a_x"].reshape(16, 2048, 1024))
    shards = [np.ascontiguousarray(x[2 * c:2 * c + 2].reshape(NTOK, D))
              for c in range(N_CORES)]
    ident = np.eye(128, dtype=BF16NP)
    for si, (builder, wspecs, hostw) in enumerate(SUBLAYER_DEFS):
        ck = builder.__name__
        if ck not in _NC_CACHE:
            _NC_CACHE[ck] = build_sublayer_nc(builder, wspecs)
        nc, _ = _NC_CACHE[ck]
        warr = hostw(inputs)
        warr["c_ident"] = ident
        in_maps = [dict(warr, x_in=shards[c]) for c in range(N_CORES)]
        res = run_bass_kernel_spmd(nc, in_maps, core_ids=list(range(N_CORES)))
        shards = [np.ascontiguousarray(res.results[c]["x_out"])
                  for c in range(N_CORES)]
    out = np.stack([s.reshape(2, 2048, 1024) for s in shards])
    return out.reshape(16, 2048, 1024)


def build_fused_nc():
    nc = bass.Bass("TRN2", target_bir_lowering=False)
    x_in = nc.dram_tensor("x_in", [NTOK, D], F32, kind="ExternalInput").ap()
    x_out = nc.dram_tensor("x_out", [NTOK, D], F32, kind="ExternalOutput").ap()
    t0 = nc.dram_tensor("xs0", [NTOK, D], F32, kind="Internal").ap()
    t1 = nc.dram_tensor("xs1", [NTOK, D], F32, kind="Internal").ap()
    ident_d = nc.dram_tensor("c_ident", [128, 128], BF16, kind="ExternalInput").ap()
    wgroups = {}
    for builder, wspecs, hostw in SUBLAYER_DEFS:
        bn = builder.__name__
        if bn not in wgroups:
            w = {name: nc.dram_tensor(f"{bn}_{name}", shape, F32,
                                      kind="ExternalInput").ap()
                 for name, shape in wspecs}
            w["c_ident"] = ident_d
            wgroups[bn] = w
    p = Prog(nc)
    ident = load_ident(p, {"c_ident": ident_d})
    chain = [x_in, t0, t1, t0, t1, t0, t1, t0, x_out]
    for si, (builder, wspecs, hostw) in enumerate(SUBLAYER_DEFS):
        p.push_scope()
        builder(p, ident, wgroups[builder.__name__], chain[si], chain[si + 1],
                uname=f"s{si}")
        p.pop_scope()
    n = p.emit_all()
    return nc, n


_FUSED_CACHE = []


def kernel_fused(**inputs):
    x = np.ascontiguousarray(inputs["L0_attn_a_x"].reshape(16, 2048, 1024))
    shards = [np.ascontiguousarray(x[2 * c:2 * c + 2].reshape(NTOK, D))
              for c in range(N_CORES)]
    if not _FUSED_CACHE:
        _FUSED_CACHE.append(build_fused_nc())
    nc, _ = _FUSED_CACHE[0]
    warr = {"c_ident": np.eye(128, dtype=BF16NP)}
    seen = set()
    for builder, wspecs, hostw in SUBLAYER_DEFS:
        bn = builder.__name__
        if bn in seen:
            continue
        seen.add(bn)
        for name, arr in hostw(inputs).items():
            warr[f"{bn}_{name}"] = arr
    in_maps = [dict(warr, x_in=shards[c]) for c in range(N_CORES)]
    res = run_bass_kernel_spmd(nc, in_maps, core_ids=list(range(N_CORES)))
    out = np.stack([res.results[c]["x_out"].reshape(2, 2048, 1024)
                    for c in range(N_CORES)])
    return out.reshape(16, 2048, 1024)


kernel_unfused = kernel
kernel = kernel_fused


# version 13
# speedup vs baseline: 1.0304x; 1.0304x over previous
import numpy as np
import ml_dtypes
import concourse.bass as bass
import concourse.mybir as mybir
from concourse.ap import AP
from concourse.bass_utils import run_bass_kernel_spmd

F32 = mybir.dt.float32
BF16 = mybir.dt.bfloat16
AF = mybir.ActivationFunctionType
ALU = mybir.AluOpType
AX = mybir.AxisListType
BF16NP = ml_dtypes.bfloat16

N_CORES = 8
B, T, D = 16, 2048, 1024
SEQ_PER_CORE = B // N_CORES
NTOK = SEQ_PER_CORE * T
NTILE = NTOK // 128
TPS = T // 128
NEG = -1e30

ENG_SECTIONS = {"pe": "tensor", "dve": "vector", "act": "scalar",
                "pool": "gpsimd", "sp": "sync"}
DMA_QUEUES = {"sp": "qsp", "act": "qact", "pool": "qpool"}


class Op:
    __slots__ = ("idx", "eng", "emit", "deps", "dma", "sem", "val", "tag", "inc")

    def __init__(self, idx, eng, emit, deps, dma, tag):
        self.idx, self.eng, self.emit, self.deps, self.dma, self.tag = \
            idx, eng, emit, deps, dma, tag
        self.sem = None
        self.val = None
        self.inc = False


class Prog:
    def __init__(self, nc):
        self.nc = nc
        self.ops = []
        self.res = {}
        self._uid = 0
        self.fence_deps = set()
        self._scopes = []
        self._consts = {}
        self.queue_dmas = {}
        self.QDEPTH = 8

    def uid(self, p):
        self._uid += 1
        return f"{p}_{self._uid}"

    def sb(self, prefix, shape, dtype):
        cm = self.nc.sbuf_tensor(self.uid(prefix), list(shape), dtype)
        h = cm.__enter__()
        if self._scopes:
            self._scopes[-1].append(cm)
        return h

    def psum(self, prefix, shape, dtype=F32):
        cm = self.nc.psum_tensor(self.uid(prefix), list(shape), dtype)
        h = cm.__enter__()
        if self._scopes:
            self._scopes[-1].append(cm)
        return h

    def constf(self, val, dtype=F32):
        ck = ("constf", float(val), dtype)
        if ck not in self._consts:
            cm = self.nc.sbuf_tensor(self.uid("cst"), [128, 1], dtype, side="right")
            h = cm.__enter__()
            key = ("cst", float(val))
            self.op("pool", lambda e, a=h, v=float(val): e.memset(a[:], v), w=[key])
            self._consts[ck] = (h[:, 0:1], key)
        return self._consts[ck]

    def push_scope(self):
        self._scopes.append([])

    def pop_scope(self):
        cms = self._scopes.pop()
        for cm in reversed(cms):
            cm.__exit__(None, None, None)
        self.barrier()

    def op(self, eng, emit, r=(), w=(), a=(), dma=False, tag="", nofence=False):
        idx = len(self.ops)
        deps = set() if nofence else set(self.fence_deps)
        for key in r:
            st = self.res.get(key)
            if st is not None and st[0] is not None:
                deps.add(st[0])
        for key in w:
            st = self.res.get(key)
            if st is not None:
                if st[0] is not None:
                    deps.add(st[0])
                deps.update(st[1])
        deps.discard(idx)
        for key in r:
            st = self.res.setdefault(key, [None, []])
            st[1].append(idx)
        for key in w:
            self.res[key] = [idx, []]
        for key in a:
            st = self.res.setdefault(key, [None, []])
            st[0] = idx
        o = Op(idx, eng, emit, deps, dma, tag)
        if dma:
            q = DMA_QUEUES[eng]
            lst = self.queue_dmas.setdefault(q, [])
            i = len(lst)
            if i >= self.QDEPTH:
                deps.add(lst[i - self.QDEPTH])
            lst.append(idx)
            o.sem = f"{q}{i % self.QDEPTH}"
            o.val = 16 * (i // self.QDEPTH + 1)
            o.inc = True
        self.ops.append(o)
        return o

    def barrier(self):
        last = {}
        for o in self.ops:
            key = ("q", o.sem) if o.dma else ("e", o.eng)
            last[key] = o.idx
        self.fence_deps = set(last.values())

    def dma(self, qeng, out_ap, in_ap, r=(), w=(), tag="dma", nofence=False):
        return self.op(qeng, lambda e, o=out_ap, i=in_ap: e.dma_start(out=o, in_=i),
                       r=r, w=w, dma=True, tag=tag, nofence=nofence)

    def mm(self, out_ap, lhsT_ap, rhs_ap, start, stop, r=(), w=(), a=(), tag="mm",
           tile_position=None):
        return self.op("pe", lambda e, o=out_ap, l=lhsT_ap, rr=rhs_ap, s1=start,
                       s2=stop, tp=tile_position:
                       e.matmul(o, l, rr, start=s1, stop=s2, tile_position=tp),
                       r=r, w=w, a=a, tag=tag)

    def mm_group(self, out_ap, prods, wkey, tag="mmg"):
        n = len(prods)
        for i, (l, rr, rk) in enumerate(prods):
            self.mm(out_ap, l, rr, i == 0, i == n - 1, r=rk,
                    w=([wkey] if i == 0 else ()), a=([wkey] if i > 0 else ()),
                    tag=tag)

    def tr(self, out_ap, in_ap, ident_ap, r=(), w=(), a=(), tag="tr"):
        return self.op("pe", lambda e, o=out_ap, i=in_ap, d=ident_ap:
                       e.transpose(o, i, d), r=list(r) + ["ident"], w=w, a=a, tag=tag)

    def copy(self, eng, out_ap, in_ap, r=(), w=(), tag="cp"):
        return self.op(eng, lambda e, o=out_ap, i=in_ap: e.tensor_copy(out=o, in_=i),
                       r=r, w=w, tag=tag)

    def tt(self, out_ap, in0, in1, alu, r=(), w=(), tag="tt", eng="dve"):
        return self.op(eng, lambda e, o=out_ap, a0=in0, a1=in1, op=alu:
                       e.tensor_tensor(out=o, in0=a0, in1=a1, op=op), r=r, w=w, tag=tag)

    def ts(self, out_ap, in0, s1, s2, op0, op1, r=(), w=(), tag="ts", eng="dve"):
        return self.op(eng, lambda e, o=out_ap, a0=in0, x1=s1, x2=s2, p0=op0, p1=op1:
                       e.tensor_scalar(out=o, in0=a0, scalar1=x1, scalar2=x2,
                                       op0=p0, op1=p1), r=r, w=w, tag=tag)

    def tsm(self, out_ap, in0, s1, r=(), w=(), tag="tsm"):
        return self.op("dve", lambda e, o=out_ap, a0=in0, x1=s1:
                       e.tensor_scalar_mul(out=o, in0=a0, scalar1=x1), r=r, w=w, tag=tag)

    def stt(self, out_ap, in0, scal, in1, op0, op1, r=(), w=(), tag="stt"):
        return self.op("dve", lambda e, o=out_ap, a0=in0, s=scal, a1=in1, p0=op0, p1=op1:
                       e.scalar_tensor_tensor(out=o, in0=a0, scalar=s, in1=a1,
                                              op0=p0, op1=p1), r=r, w=w, tag=tag)

    def red(self, out_ap, in_ap, alu, negate=False, r=(), w=(), tag="red", eng="dve"):
        return self.op(eng, lambda e, o=out_ap, i=in_ap, op=alu, n=negate:
                       e.tensor_reduce(out=o, in_=i, axis=AX.X, op=op, negate=n),
                       r=r, w=w, tag=tag)

    def recip(self, out_ap, in_ap, r=(), w=(), tag="rcp"):
        return self.op("dve", lambda e, o=out_ap, i=in_ap: e.reciprocal(out=o, in_=i),
                       r=r, w=w, tag=tag)

    def act(self, out_ap, in_ap, func, bias=0.0, scale=1.0, accum_out=None,
            r=(), w=(), tag="act"):
        return self.op("act", lambda e, o=out_ap, i=in_ap, f=func, b=bias, s=scale,
                       ac=accum_out: e.activation(out=o, in_=i, func=f, bias=b,
                                                  scale=s, accum_out=ac),
                       r=r, w=w, tag=tag)

    def acopy(self, out_ap, in_ap, scale=1.0, r=(), w=(), tag="acp"):
        return self.op("act", lambda e, o=out_ap, i=in_ap, s=scale:
                       e.activation(out=o, in_=i, func=AF.Copy, scale=s),
                       r=r, w=w, tag=tag)

    def emit_all(self):
        nc = self.nc
        awaited = set()
        for o in self.ops:
            awaited.update(o.deps)
        eng_count = {}
        queue_final = {}
        for o in self.ops:
            if o.dma:
                queue_final[o.sem] = o.val
            elif o.idx in awaited:
                eng_count[o.eng] = eng_count.get(o.eng, 0) + 1
                o.sem, o.val, o.inc = o.eng, eng_count[o.eng], True
        sem_names = sorted(set(list(eng_count) + list(queue_final)))
        sems = {s: nc.alloc_semaphore(f"sem_{s}") for s in sem_names}
        by_eng = {e: [] for e in ENG_SECTIONS}
        for o in self.ops:
            by_eng[o.eng].append(o)

        with nc.Block() as blk:
            for eng, section in ENG_SECTIONS.items():
                ops = by_eng[eng]
                if not ops:
                    continue

                def body(engine, ops=ops, eng=eng):
                    waited = {}
                    for o in ops:
                        targets = {}
                        for d in o.deps:
                            dop = self.ops[d]
                            if dop.sem is None:
                                continue
                            if targets.get(dop.sem, 0) < dop.val:
                                targets[dop.sem] = dop.val
                        for s, v in sorted(targets.items()):
                            if waited.get(s, 0) < v:
                                engine.wait_ge(sems[s], v)
                                waited[s] = v
                        inst = o.emit(engine)
                        if o.inc:
                            inst.then_inc(sems[o.sem], 16 if o.dma else 1)
                    q = DMA_QUEUES.get(eng)
                    if q:
                        for s, v in sorted(queue_final.items()):
                            if s.startswith(q) and waited.get(s, 0) < v:
                                engine.wait_ge(sems[s], v)

                getattr(blk, section)(body)
        return len(self.ops)


def fap(handle, offset, dims):
    base = handle.ap()
    part = list(base.ap)[0]
    return AP(base.tensor, offset, [list(part)] + [list(d) for d in dims])


def fm_cols(t_total, c, lo, n):
    return slice(c * t_total + lo, c * t_total + lo + n)



def load_ident(p, w):
    ident = p.sb("ident", [128, 128], BF16)
    p.dma("sp", ident[:], w["c_ident"], w=["ident"], tag="ld_ident")
    return ident


def emit_ln_to_fm(p, ident, x_dram_fn, tiles, xn_fm, xn_key, t_total,
                  lo_fm=None, lo_key=None, tag="ln",
                  normw_brd=None, xln_sb=None, xln_key=None,
                  per_tile_hook=None, pt=None):
    want_lo = lo_fm is not None
    want_f32 = want_lo or (normw_brd is not None)
    if pt is None:
        pt = [p.psum(f"{tag}pt", [128, 1024], BF16) for _ in range(2)]
    xt = [p.sb(f"{tag}xt", [128, 1024], F32) for _ in range(2)]
    st6 = [p.sb(f"{tag}s6", [128, 12], F32) for _ in range(2)]
    mv = [p.sb(f"{tag}mv", [128, 2], F32) for _ in range(2)]
    sd = [p.sb(f"{tag}sd", [128, 1], F32) for _ in range(2)]
    xnt = [p.sb(f"{tag}xb", [128, 1024], BF16) for _ in range(2)]
    xf32 = [p.sb(f"{tag}xf", [128, 1024], F32) for _ in range(2)] if want_f32 else None
    xlo = [p.sb(f"{tag}xl", [128, 1024], BF16) for _ in range(2)] if want_lo else None
    ms = [p.sb(f"{tag}ms", [128, 2], F32) for _ in range(2)] if normw_brd is not None else None

    for j, gi in tiles:
        pr = j % 2
        kxt, ks6, kmv, ksd, kxb = [f"{tag}{n}{pr}" for n in ("xt", "s6", "mv", "sd", "xb")]
        p.dma("sp", xt[pr][:], x_dram_fn(gi), w=[kxt], tag=f"{tag}ldx{gi}")
        p.op("dve", lambda e, a=st6[pr], b=xt[pr]: e.bn_stats(out=a[:, 0:6], in_=b[:, 0:512]),
             r=[kxt], w=[ks6])
        p.op("dve", lambda e, a=st6[pr], b=xt[pr]: e.bn_stats(out=a[:, 6:12], in_=b[:, 512:1024]),
             r=[kxt], w=[ks6])
        p.op("dve", lambda e, a=mv[pr], b=st6[pr]: e.bn_aggr(out=a[:, 0:2], in_=b[:]),
             r=[ks6], w=[kmv])
        eps_ap, eps_key = p.constf(1e-5)
        p.act(sd[pr][:], mv[pr][:, 1:2], AF.Sqrt, bias=eps_ap, r=[kmv, eps_key], w=[ksd])
        p.recip(sd[pr][:], sd[pr][:], r=[ksd], w=[ksd])
        if want_f32:
            kxf = f"{tag}xf{pr}"
            p.ts(xf32[pr][:], xt[pr][:], mv[pr][:, 0:1], sd[pr][:, 0:1],
                 ALU.subtract, ALU.mult, r=[kxt, kmv, ksd], w=[kxf])
            if normw_brd is not None:
                if xln_sb is not None:
                    p.copy("dve", xln_sb[:, 1024 * j:1024 * (j + 1)], xf32[pr][:],
                           r=[kxf], w=[(xln_key, j)])
                kms = f"{tag}ms{pr}"
                p.op("dve", lambda e, a=st6[pr], b=xf32[pr]: e.bn_stats(out=a[:, 0:6], in_=b[:, 0:512]),
                     r=[kxf], w=[ks6])
                p.op("dve", lambda e, a=st6[pr], b=xf32[pr]: e.bn_stats(out=a[:, 6:12], in_=b[:, 512:1024]),
                     r=[kxf], w=[ks6])
                p.op("dve", lambda e, a=ms[pr], b=st6[pr]: e.bn_aggr(out=a[:, 0:2], in_=b[:]),
                     r=[ks6], w=[kms])
                p.tt(ms[pr][:, 0:1], ms[pr][:, 0:1], ms[pr][:, 0:1], ALU.mult,
                     r=[kms], w=[kms])
                p.tt(ms[pr][:, 1:2], ms[pr][:, 1:2], ms[pr][:, 0:1], ALU.add,
                     r=[kms], w=[kms])
                eps2_ap, eps2_key = p.constf(1.1920929e-07)
                p.act(sd[pr][:], ms[pr][:, 1:2], AF.Sqrt, bias=eps2_ap,
                      r=[kms, eps2_key], w=[ksd])
                p.recip(sd[pr][:], sd[pr][:], r=[ksd], w=[ksd])
                p.stt(xnt[pr][:], xf32[pr][:], sd[pr][:, 0:1], normw_brd[:],
                      ALU.mult, ALU.mult, r=[kxf, ksd, "normw_brd"], w=[kxb])
            else:
                p.copy("dve", xnt[pr][:], xf32[pr][:], r=[kxf], w=[kxb])
        else:
            p.ts(xnt[pr][:], xt[pr][:], mv[pr][:, 0:1], sd[pr][:, 0:1],
                 ALU.subtract, ALU.mult, r=[kxt, kmv, ksd], w=[kxb])
        if want_lo:
            kxl = f"{tag}xl{pr}"
            p.tt(xlo[pr][:], xf32[pr][:], xnt[pr][:], ALU.subtract,
                 r=[f"{tag}xf{pr}", kxb], w=[kxl])
        srcs = [(xnt[pr], kxb, xn_fm, xn_key)]
        if want_lo:
            srcs.append((xlo[pr], f"{tag}xl{pr}", lo_fm, lo_key))
        for src, skey, dst, dkey in srcs:
            kbank = f"{tag}ptb{pr}"
            for c in range(8):
                p.tr(pt[pr][:, 128 * c:128 * (c + 1)], src[:, 128 * c:128 * (c + 1)],
                     ident[:], r=[skey], w=([kbank] if c == 0 else ()),
                     a=([kbank] if c > 0 else ()))
            dst_ap = fap(dst, 128 * j, [[t_total, 8], [1, 128]])
            p.copy("dve", dst_ap, pt[pr][:], r=[kbank], w=[(dkey, j)],
                   tag=f"{tag}evac{gi}")
        if per_tile_hook is not None:
            per_tile_hook(j, gi)


def emit_router(p, z_sb, zkey, col0, L, k, out_ap, out_key, scale_out, rsc, tag="rt"):
    zc = z_sb[:, col0:col0 + L]
    m1, mk, g, zk, zk2, cp, b3, sm = (rsc[n] for n in
                                      ("m1", "mk", "g", "zk", "zk2", "cp", "b3", "sum"))
    K = lambda n: f"{tag}{n}"
    p.red(m1[:], zc, ALU.max, r=[zkey], w=[K("m1")])
    cur, kcur = zc, [zkey]
    curm, kcurm = m1, K("m1")
    for step in range(k - 1):
        nxt, knxt = (zk, K("zk")) if step == 0 else (zk2, K("zk2"))
        p.ts(g[:, 0:L], cur, curm[:, 0:1], None, ALU.is_ge, ALU.bypass,
             r=kcur + [kcurm], w=[K("g")])
        p.stt(nxt[:, 0:L], g[:, 0:L], NEG, cur, ALU.mult, ALU.add,
              r=[K("g")] + kcur, w=[knxt])
        cur, kcur = nxt[:, 0:L], [knxt]
        p.red(mk[:], cur, ALU.max, r=kcur, w=[K("mk") + str(step)])
        curm, kcurm = mk, K("mk") + str(step)
    p.ts(g[:, 0:L], zc, curm[:, 0:1], None, ALU.is_ge, ALU.bypass,
         r=[zkey, kcurm], w=[K("g")])
    p.tsm(b3[:], m1[:], -1.0, r=[K("m1")], w=[K("b3")])
    p.act(cp[:, 0:L], zc, AF.Exp, bias=b3[:, 0:1], scale=1.0,
          r=[zkey, K("b3")], w=[K("cp")])
    p.tt(cp[:, 0:L], cp[:, 0:L], g[:, 0:L], ALU.mult, r=[K("cp"), K("g")], w=[K("cp")])
    p.red(sm[:], cp[:, 0:L], ALU.add, r=[K("cp")], w=[K("sum")])
    p.recip(sm[:], sm[:], r=[K("sum")], w=[K("sum")])
    if scale_out != 1.0:
        p.tsm(sm[:], sm[:], scale_out, r=[K("sum")], w=[K("sum")])
    p.ts(out_ap, cp[:, 0:L], sm[:, 0:1], None, ALU.mult, ALU.bypass,
         r=[K("cp"), K("sum")], w=[out_key])


def router_scratch(p, L):
    rsc = {}
    for n in ("g", "zk", "zk2", "cp"):
        rsc[n] = p.sb(f"r{n}", [128, L], F32)
    for n in ("m1", "mk", "b3", "sum"):
        rsc[n] = p.sb(f"r{n}", [128, 1], F32)
    return rsc



def build_l0_ffn(p, ident, w, x_in, x_out, uname=""):
    GT = 2048
    GTILES = GT // 128

    p.push_scope()
    pw = p.sb("pw", [128, 512], F32)
    p.dma("sp", pw[:], w["P_wT"].rearrange("(c p) a -> p c a", p=128),
          w=["pw"], tag="ld_pw")
    pwh = p.sb("pwh", [128, 512], BF16)
    pwl = p.sb("pwl", [128, 512], BF16)
    p.copy("dve", pwh[:], pw[:], r=["pw"], w=["pwh"])
    p.tt(pwl[:], pw[:], pwh[:], ALU.subtract, r=["pw", "pwh"], w=["pwl"])
    uc = p.sb("uc", [64, 36], F32)
    p.dma("sp", uc[:], w["U_cat"], w=["uc"], tag="ld_uc")
    uch = p.sb("uch", [64, 36], BF16)
    ucl = p.sb("ucl", [64, 36], BF16)
    p.copy("dve", uch[:], uc[:], r=["uc"], w=["uch"])
    p.tt(ucl[:], uc[:], uch[:], ALU.subtract, r=["uc", "uch"], w=["ucl"])
    b2 = p.sb("b2", [12, 1024], BF16)
    p.dma("pool", b2[:], w["b2"], w=["b2"], tag="ld_b2")

    for g in range(NTOK // GT):
        p.push_scope()
        xn = p.sb("xn", [128, 8 * GT], BF16)
        c1 = p.sb("c1", [128, 12 * GTILES], F32)
        c2 = p.sb("c2", [128, 12 * GTILES], F32)
        c3f = p.sb("c3f", [12, 128 * GTILES], BF16)

        p.push_scope()
        xn_lo = p.sb("xnlo", [128, 8 * GT], BF16)
        zps = p.psum("zps", [128, 64], F32)
        xaps = p.psum("xaps", [64, 128], F32)
        ptc3 = p.psum("ptc3", [128, 128], BF16)
        z_sb = p.sb("z", [128, 36], F32)
        xah = p.sb("xah", [64, 128], BF16)
        xal = p.sb("xal", [64, 128], BF16)
        rsc = router_scratch(p, 12)
        c3w = p.sb("c3w", [128, 12], F32)
        c3t = p.sb("c3t", [128, 12], BF16)

        def hook(j, gi, g=g, xn=xn, xn_lo=xn_lo, c1=c1, c2=c2, c3f=c3f,
                 zps=zps, xaps=xaps, ptc3=ptc3, z_sb=z_sb, xah=xah, xal=xal,
                 rsc=rsc, c3w=c3w, c3t=c3t):
            tg = f"{uname}g{g}t{j}"
            prods = []
            for c in range(8):
                xn_c = xn[:, fm_cols(GT, c, 128 * j, 128)]
                lo_c = xn_lo[:, fm_cols(GT, c, 128 * j, 128)]
                pc = pw_slice = slice(64 * c, 64 * (c + 1))
                prods.append((pwh[:, pc], xn_c, ["pwh", ("xnfm", j)]))
                prods.append((pwl[:, pc], xn_c, ["pwl", ("xnfm", j)]))
                prods.append((pwh[:, pc], lo_c, ["pwh", ("xnlo", j)]))
            p.mm_group(xaps[:], prods, "xaps", tag=f"xa{tg}")
            p.copy("dve", xah[:], xaps[:], r=["xaps"], w=["xah"])
            p.tt(xal[:], xaps[:], xah[:], ALU.subtract, r=["xaps", "xah"], w=["xal"])
            p.mm_group(zps[:, 0:36],
                       [(xah[:], uch[:], ["xah", "uch"]),
                        (xah[:], ucl[:], ["xah", "ucl"]),
                        (xal[:], uch[:], ["xal", "uch"])], "zps", tag=f"z{tg}")
            p.copy("dve", z_sb[:], zps[:, 0:36], r=["zps"], w=["zsb"])
            emit_router(p, z_sb, "zsb", 0, 12, 3, c1[:, 12 * j:12 * (j + 1)],
                        ("c1", j), 1.0, rsc, tag=f"{tg}r0")
            emit_router(p, z_sb, "zsb", 12, 12, 3, c2[:, 12 * j:12 * (j + 1)],
                        ("c2", j), 1.0, rsc, tag=f"{tg}r1")
            emit_router(p, z_sb, "zsb", 24, 12, 3, c3w[:, 0:12],
                        "c3w", 1.0, rsc, tag=f"{tg}r2")
            p.copy("dve", c3t[:], c3w[:], r=["c3w"], w=["c3t"])
            p.tr(ptc3[0:12, :], c3t[:, 0:12], ident[:], r=["c3t"], w=["ptc3"])
            p.copy("dve", c3f[:, 128 * j:128 * (j + 1)], ptc3[0:12, :],
                   r=["ptc3"], w=[("c3f", j)])

        emit_ln_to_fm(p, ident, lambda gi: x_in[128 * gi:128 * (gi + 1), :],
                      [(j, g * GTILES + j) for j in range(GTILES)],
                      xn, "xnfm", GT, lo_fm=xn_lo, lo_key="xnlo",
                      tag=f"{uname}lnf{g}_", per_tile_hook=hook)
        p.pop_scope()

        p.push_scope()
        acc = p.sb("acc", [128, 1024 * GTILES], F32)
        h_fm = p.sb("hfm", [128, 8 * GT], BF16)
        wsl = [p.sb("wsl", [128, 8 * 1024], BF16) for _ in range(2)]
        msp = [p.psum("msp", [128, 512], F32) for _ in range(2)]
        pth = [p.psum("pth", [128, 1024], BF16) for _ in range(2)]
        htile = [p.sb("ht", [128, 1024], BF16)]

        def w_fm_src(name, e):
            return w[name][e].rearrange("(c p) h -> p c h", p=128)

        def mix(src_fm, src_key, wname, cmat, ckey, init):
            for e in range(12):
                sl = e % 2
                kw = f"wslot{sl}"
                p.dma("pool", wsl[sl][:], w_fm_src(wname, e), w=[kw],
                      tag=f"ld{wname}{e}", nofence=(e >= 2))
                for j in range(GTILES):
                    for dg in range(2):
                        pi = (j * 2 + dg) % 2
                        ps, kps = msp[pi], f"msp{pi}"
                        prods = [(src_fm[:, fm_cols(GT, c, 128 * j, 128)],
                                  wsl[sl][:, 1024 * c + 512 * dg:1024 * c + 512 * (dg + 1)],
                                  [(src_key, j), kw]) for c in range(8)]
                        p.mm_group(ps[:], prods, kps, tag=f"{wname}e{e}j{j}")
                        dst = acc[:, 1024 * j + 512 * dg:1024 * j + 512 * (dg + 1)]
                        kacc = ("acc", j, dg)
                        cs = cmat[:, 12 * j + e:12 * j + e + 1]
                        if init and e == 0:
                            p.ts(dst, ps[:], cs, None, ALU.mult, ALU.bypass,
                                 r=[kps, (ckey, j)], w=[kacc])
                        else:
                            p.stt(dst, ps[:], cs, dst, ALU.mult, ALU.add,
                                  r=[kps, (ckey, j), kacc], w=[kacc])

        mix(xn, "xnfm", "W1T", c1, "c1", init=True)
        for j in range(GTILES):
            pr = j % 2
            p.act(htile[0][:], acc[:, 1024 * j:1024 * (j + 1)], AF.Gelu,
                  r=[("acc", j, 0), ("acc", j, 1)], w=["ht0"], tag=f"gelu{j}")
            kbank = f"pth{pr}"
            for c in range(8):
                p.tr(pth[pr][:, 128 * c:128 * (c + 1)],
                     htile[0][:, 128 * c:128 * (c + 1)], ident[:],
                     r=["ht0"], w=([kbank] if c == 0 else ()),
                     a=([kbank] if c > 0 else ()))
            p.copy("dve", fap(h_fm, 128 * j, [[GT, 8], [1, 128]]), pth[pr][:],
                   r=[kbank], w=[("hfm", j)])

        xt2 = [p.sb("xt2", [128, 1024], F32) for _ in range(2)]
        for j in range(GTILES):
            pr = j % 2
            kx = f"xt2{pr}"
            gi = g * GTILES + j
            p.dma("sp", xt2[pr][:], x_in[128 * gi:128 * (gi + 1), :], w=[kx],
                  tag=f"ldx2{gi}")
            for dg in range(2):
                pi = (j * 2 + dg) % 2
                ps, kps = msp[pi], f"msp{pi}"
                p.mm(ps[:], c3f[:, 128 * j:128 * (j + 1)],
                     b2[:, 512 * dg:512 * (dg + 1)], True, True,
                     r=[("c3f", j), "b2"], w=[kps], tag=f"b2j{j}")
                dst = acc[:, 1024 * j + 512 * dg:1024 * j + 512 * (dg + 1)]
                p.tt(dst, ps[:], xt2[pr][:, 512 * dg:512 * (dg + 1)], ALU.add,
                     r=[kps, kx], w=[("acc", j, dg)])

        mix(h_fm, "hfm", "W2T", c2, "c2", init=False)
        for j in range(GTILES):
            gi = g * GTILES + j
            p.dma("sp", x_out[128 * gi:128 * (gi + 1), :],
                  acc[:, 1024 * j:1024 * (j + 1)],
                  r=[("acc", j, 0), ("acc", j, 1)], tag=f"stx{gi}")
        p.pop_scope()
        p.pop_scope()
    p.pop_scope()


def l0_ffn_host_weights(inputs):
    return {
        "P_wT": np.ascontiguousarray(inputs["L0_ffn_P_w"].T),
        "U_cat": np.ascontiguousarray(np.concatenate(
            [inputs["L0_ffn_U1"].T, inputs["L0_ffn_U2"].T, inputs["L0_ffn_U3"].T],
            axis=1)),
        "W1T": np.ascontiguousarray(inputs["L0_ffn_W1"].transpose(0, 2, 1)),
        "W2T": np.ascontiguousarray(inputs["L0_ffn_W2"].transpose(0, 2, 1)),
        "b2": np.ascontiguousarray(inputs["L0_ffn_b2"]),
    }


WSPECS_L0_FFN = [("P_wT", [1024, 64]), ("U_cat", [64, 36]),
                 ("W1T", [12, 1024, 1024]), ("W2T", [12, 1024, 1024]),
                 ("b2", [12, 1024])]



class LnTilePool:

    def __init__(self, p, tag, rms=False):
        self.p, self.tag, self.rms = p, tag, rms
        self.xt = [p.sb(f"{tag}xt", [128, 1024], F32) for _ in range(2)]
        self.s6 = [p.sb(f"{tag}s6", [128, 12], F32) for _ in range(2)]
        self.mv = [p.sb(f"{tag}mv", [128, 2], F32) for _ in range(2)]
        self.sd = [p.sb(f"{tag}sd", [128, 1], F32) for _ in range(2)]
        self.xb = [p.sb(f"{tag}xb", [128, 1024], BF16) for _ in range(2)]
        self.xf = [p.sb(f"{tag}xf", [128, 1024], F32) for _ in range(2)] if rms else None
        self.ms = [p.sb(f"{tag}ms", [128, 2], F32) for _ in range(2)] if rms else None
        self.pt = [p.psum(f"{tag}pt", [128, 1024], BF16) for _ in range(2)]


def emit_ln_tile(p, ident, pool, j, x_ap, dst_fm_ap, dst_key, normw=None):
    pr = j % 2
    tag = pool.tag
    kxt, ks6, kmv, ksd, kxb = [f"{tag}{n}{pr}" for n in ("xt", "s6", "mv", "sd", "xb")]
    xt, s6, mv, sd, xb = (pool.xt[pr], pool.s6[pr], pool.mv[pr], pool.sd[pr],
                          pool.xb[pr])
    p.dma("sp", xt[:], x_ap, w=[kxt], tag=f"{tag}ldx")
    p.op("dve", lambda e, a=s6, b=xt: e.bn_stats(out=a[:, 0:6], in_=b[:, 0:512]),
         r=[kxt], w=[ks6])
    p.op("dve", lambda e, a=s6, b=xt: e.bn_stats(out=a[:, 6:12], in_=b[:, 512:1024]),
         r=[kxt], w=[ks6])
    p.op("dve", lambda e, a=mv, b=s6: e.bn_aggr(out=a[:, 0:2], in_=b[:]),
         r=[ks6], w=[kmv])
    eps_ap, eps_key = p.constf(1e-5)
    p.act(sd[:], mv[:, 1:2], AF.Sqrt, bias=eps_ap, r=[kmv, eps_key], w=[ksd])
    p.recip(sd[:], sd[:], r=[ksd], w=[ksd])
    if normw is None:
        p.ts(xb[:], xt[:], mv[:, 0:1], sd[:, 0:1], ALU.subtract, ALU.mult,
             r=[kxt, kmv, ksd], w=[kxb])
    else:
        xf, ms = pool.xf[pr], pool.ms[pr]
        kxf, kms = f"{tag}xf{pr}", f"{tag}ms{pr}"
        p.ts(xf[:], xt[:], mv[:, 0:1], sd[:, 0:1], ALU.subtract, ALU.mult,
             r=[kxt, kmv, ksd], w=[kxf])
        p.op("dve", lambda e, a=s6, b=xf: e.bn_stats(out=a[:, 0:6], in_=b[:, 0:512]),
             r=[kxf], w=[ks6])
        p.op("dve", lambda e, a=s6, b=xf: e.bn_stats(out=a[:, 6:12], in_=b[:, 512:1024]),
             r=[kxf], w=[ks6])
        p.op("dve", lambda e, a=ms, b=s6: e.bn_aggr(out=a[:, 0:2], in_=b[:]),
             r=[ks6], w=[kms])
        p.tt(ms[:, 0:1], ms[:, 0:1], ms[:, 0:1], ALU.mult, r=[kms], w=[kms])
        p.tt(ms[:, 1:2], ms[:, 1:2], ms[:, 0:1], ALU.add, r=[kms], w=[kms])
        eps2_ap, eps2_key = p.constf(1.1920929e-07)
        p.act(sd[:], ms[:, 1:2], AF.Sqrt, bias=eps2_ap, r=[kms, eps2_key], w=[ksd])
        p.recip(sd[:], sd[:], r=[ksd], w=[ksd])
        p.stt(xb[:], xf[:], sd[:, 0:1], normw[0][:], ALU.mult, ALU.mult,
              r=[kxf, ksd, normw[1]], w=[kxb])
    kbank = f"{tag}ptb{pr}"
    for c in range(8):
        p.tr(pool.pt[pr][:, 128 * c:128 * (c + 1)], xb[:, 128 * c:128 * (c + 1)],
             ident[:], r=[kxb], w=([kbank] if c == 0 else ()),
             a=([kbank] if c > 0 else ()))
    p.acopy(dst_fm_ap, pool.pt[pr][:], r=[kbank], w=[dst_key])


class PtPool:

    def __init__(self, p, tag):
        self.tag = tag
        self.pt = [p.psum(f"{tag}pt", [128, 1024], BF16) for _ in range(2)]


def transpose_tile_to(p, ident, pool, pr, src_sb, src_key, dst_ap, dst_key, nch=8):
    tag = pool.tag
    kbank = f"{tag}ptb{pr}"
    for c in range(nch):
        p.tr(pool.pt[pr][:, 128 * c:128 * (c + 1)], src_sb[:, 128 * c:128 * (c + 1)],
             ident[:], r=[src_key], w=([kbank] if c == 0 else ()),
             a=([kbank] if c > 0 else ()))
    p.acopy(dst_ap, pool.pt[pr][:, 0:128 * nch], r=[kbank], w=[dst_key])


def emit_attn_core(p, tag, ident, *, kind, suffix, QGT, nheads, q_ap, k_ap, v_ap,
                   out_base, attn_write, scale, maskv, TT_=2048):
    NQG = TPS // QGT
    QGW = QGT * 128
    sps = [p.psum(f"{tag}sps", [128, 512], F32) for _ in range(4)]
    ptp = [p.psum(f"{tag}ptp", [128, 1024], BF16) for _ in range(2)]
    pvp = [p.psum(f"{tag}pvp", [128, QGW], F32) for _ in range(2)]
    S_sb2 = [[p.sb(f"{tag}S", [128, TT_], F32) for _ in range(QGT)] for _ in range(2)]
    P_sb2 = [[p.sb(f"{tag}P", [128, TT_], BF16) for _ in range(QGT)] for _ in range(2)]
    PT_sb2 = [p.sb(f"{tag}PT", [128, (TT_ // 128) * QGW], BF16) for _ in range(2)]
    msc2 = [[p.sb(f"{tag}m", [128, 1], F32) for _ in range(QGT)] for _ in range(2)]
    lsc2 = [[p.sb(f"{tag}l", [128, 1], F32) for _ in range(QGT)] for _ in range(2)]
    cmx2 = [[p.sb(f"{tag}c", [128, 4], F32) for _ in range(QGT)] for _ in range(2)]
    iters = [(h, g) for h in range(nheads) for g in range(NQG)]

    def geom(g):
        f_own = (g * QGT) // 4
        if suffix:
            kcs = list(range(f_own, TT_ // 512))
            col0 = 512 * f_own
        else:
            kcs = list(range(0, f_own + 1))
            col0 = 0
        return kcs, col0, 512 * len(kcs)

    def stage_a(it):
        h, g = iters[it]
        pp = it % 2
        S_sb, msc, lsc, cmx = S_sb2[pp], msc2[pp], lsc2[pp], cmx2[pp]
        tagp = f"{tag}{pp}"
        kcs, col0, L = geom(g)
        for si in range(QGT):
            t128 = g * QGT + si
            lhs, lrk = q_ap(h, t128)
            kc_all = [(f"{tagp}S", si, kc) for kc in kcs]
            for ki, kc in enumerate(kcs):
                pi = (si + ki) % 4
                ps, kps = sps[pi], f"{tag}sps{pi}"
                kS = (f"{tagp}S", si, kc)
                rhs, rrk = k_ap(h, kc)
                p.mm(ps[:], lhs, rhs, True, True, r=lrk + rrk, w=[kps],
                     tag=f"{tag}sc")
                dst = S_sb[si][:, 512 * kc - col0:512 * kc - col0 + 512]
                partial = (kc == t128 // 4)
                if kind == "softmax":
                    if partial:
                        mp, mk = maskv[t128 % 4]
                        if isinstance(scale, tuple):
                            p.stt(dst, ps[:], scale[0], mp, ALU.mult, ALU.add,
                                  r=[kps, scale[1], mk], w=[kS])
                        else:
                            p.stt(dst, ps[:], float(scale), mp, ALU.mult,
                                  ALU.add, r=[kps, mk], w=[kS])
                    else:
                        sc = scale[0] if isinstance(scale, tuple) else float(scale)
                        rr = [kps, scale[1]] if isinstance(scale, tuple) else [kps]
                        p.acopy(dst, ps[:], scale=sc, r=rr, w=[kS])
                else:
                    if partial:
                        mp, mk = maskv[t128 % 4]
                        p.tt(dst, ps[:], mp, ALU.mult, r=[kps, mk], w=[kS])
                    else:
                        p.acopy(dst, ps[:], r=[kps], w=[kS])
                alu = ALU.max if kind == "softmax" else ALU.add
                p.red(cmx[si][:, ki:ki + 1], dst, alu, r=[kS],
                      w=[(f"{tagp}c", si)])
            km, kl = f"{tagp}m{si}", f"{tagp}l{si}"
            if kind == "softmax":
                p.red(msc[si][:], cmx[si][:, 0:len(kcs)], ALU.max, negate=True,
                      r=[(f"{tagp}c", si)], w=[km])
            else:
                p.red(lsc[si][:], cmx[si][:, 0:len(kcs)], ALU.add,
                      r=[(f"{tagp}c", si)], w=[kl])
                p.op("dve", lambda e, a=lsc[si]: e.tensor_scalar_add(
                     out=a[:], in0=a[:], scalar1=1e-6), r=[kl], w=[kl])

    def stage_b(it):
        h, g = iters[it]
        pp = it % 2
        S_sb, P_sb, PT_sb, msc, lsc = (S_sb2[pp], P_sb2[pp], PT_sb2[pp],
                                       msc2[pp], lsc2[pp])
        tagp = f"{tag}{pp}"
        kcs, col0, L = geom(g)
        nkt = L // 128
        for si in range(QGT):
            kS_all = [(f"{tagp}S", si, kc) for kc in kcs]
            km, kl = f"{tagp}m{si}", f"{tagp}l{si}"
            if kind == "softmax":
                p.op("act", lambda e, a=S_sb[si], b=msc[si], c=lsc[si], LL=L:
                     e.activation(out=a[:, 0:LL], in_=a[:, 0:LL], func=AF.Exp,
                                  bias=b[:, 0:1], scale=1.0, accum_out=c[:]),
                     r=kS_all + [km], w=kS_all + [kl])
            p.recip(lsc[si][:], lsc[si][:], r=[kl], w=[kl])
            p.acopy(P_sb[si][:, 0:L], S_sb[si][:, 0:L], scale=lsc[si][:, 0:1],
                    r=kS_all + [kl], w=[(f"{tagp}P", si)])
        for kbt in range(nkt):
            pi = kbt % 2
            kbank = f"{tag}ptpb{pi}"
            for si in range(QGT):
                p.tr(ptp[pi][:, 128 * si:128 * (si + 1)],
                     P_sb[si][:, 128 * kbt:128 * (kbt + 1)], ident[:],
                     r=[(f"{tagp}P", si)], w=([kbank] if si == 0 else ()),
                     a=([kbank] if si > 0 else ()))
            p.copy("dve", PT_sb[:, QGW * kbt:QGW * (kbt + 1)],
                   ptp[pi][:, 0:QGW], r=[kbank], w=[(f"{tagp}PT", kbt)])
        b = out_base(h)
        pi = it % 2
        pv, kpv = pvp[pi], f"{tag}pvp{pi}"
        tp = (0, 64) if b == 64 else None
        for kbt in range(nkt):
            vap, vrk = v_ap(h, col0 // 128 + kbt)
            p.mm(pv[b:b + 64, :], vap, PT_sb[:, QGW * kbt:QGW * (kbt + 1)],
                 kbt == 0, kbt == nkt - 1,
                 r=vrk + [(f"{tagp}PT", kbt)],
                 w=([kpv] if kbt == 0 else ()), a=([kpv] if kbt > 0 else ()),
                 tag=f"{tag}pv", tile_position=tp)
        attn_write(h, g, pv[b:b + 64, :], kpv)

    stage_a(0)
    for it in range(1, len(iters)):
        stage_a(it)
        stage_b(it - 1)
    stage_b(len(iters) - 1)


def emit_oproj(p, tag, ident, attn_fm, afm_rkeys, wo_sb, wo_key, x_in, x_out,
               seq0_tile, epilogue, QGT):
    ops = [p.psum(f"{tag}ops", [128, 512], F32) for _ in range(2)]
    xt2 = [p.sb(f"{tag}xt2", [128, 1024], F32) for _ in range(2)]
    ot = [p.sb(f"{tag}ot", [128, 1024], F32) for _ in range(2)]
    for j in range(TPS):
        gi = seq0_tile + j
        pr = j % 2
        kx, ko = f"{tag}xt2{pr}", f"{tag}ot{pr}"
        p.dma("sp", xt2[pr][:], x_in[128 * gi:128 * (gi + 1), :], w=[kx],
              tag=f"{tag}ldx2")
        for dg in range(2):
            pi = (2 * j + dg) % 2
            ps, kps = ops[pi], f"{tag}ops{pi}"
            prods = []
            for c in range(8):
                prods.append((attn_fm[:, 2048 * c + 128 * j:2048 * c + 128 * (j + 1)],
                              wo_sb[:, 1024 * c + 512 * dg:1024 * c + 512 * (dg + 1)],
                              afm_rkeys(c, j) + [wo_key]))
            p.mm_group(ps[:], prods, kps, tag=f"{tag}op")
            epilogue(j, dg, ps[:], kps, xt2[pr], kx,
                     ot[pr][:, 512 * dg:512 * (dg + 1)], ko)
        p.dma("sp", x_out[128 * gi:128 * (gi + 1), :], ot[pr][:],
              r=[ko], tag=f"{tag}stx")


def load_w_fm(p, qeng, dram_ap, dout, name_key, dtype=BF16):
    sb = p.sb("wfm", [128, 8 * dout], dtype)
    src = dram_ap.rearrange("(c p) d -> p c d", p=128)
    p.dma(qeng, sb[:], src, w=[name_key], tag=f"ld_{name_key}")
    return sb



def build_l0_attn_a(p, ident, w, x_in, x_out, uname=""):
    scl = p.sb("ascl", [128, 1], F32)
    p.dma("sp", scl[:], w["attn_scale_rep"], w=["ascl"], tag="ld_ascl")
    maskv = []
    for v in range(4):
        m = p.sb("amask", [128, 512], F32)
        p.dma("sp", m[:], w["maskv"][v * 128:(v + 1) * 128, :], w=[("amask", v)],
              tag="ld_mask")
        maskv.append((m[:], ("amask", v)))
    for s in range(SEQ_PER_CORE):
        p.push_scope()
        q_fm = p.sb("qfm", [128, 8 * T], BF16)
        k_fm = p.sb("kfm", [128, 8 * T], BF16)
        v_tok = p.sb("vtok", [128, TPS * 1024], BF16)
        s0 = s * TPS

        p.push_scope()
        pool = LnTilePool(p, f"{uname}qv{s}_")
        xnf = p.sb("xnf", [128, 8 * 128], BF16)
        qkps = [p.psum("qkps", [128, 512], F32) for _ in range(2)]
        qtk = [p.sb("qtk", [128, 1024], BF16) for _ in range(2)]
        wq = load_w_fm(p, "pool", w["WqT"], 1024, "wq")
        wk = load_w_fm(p, "pool", w["WkT"], 1024, "wk")
        wv = load_w_fm(p, "pool", w["WvT"], 1024, "wv")
        bqb = p.sb("bqb", [128, 1024], F32)
        p.dma("sp", bqb[:], w["bq_brd"], w=["bqb"], tag="ld_bq")
        bkb = p.sb("bkb", [128, 1024], F32)
        p.dma("sp", bkb[:], w["bk_brd"], w=["bkb"], tag="ld_bk")
        bvb = p.sb("bvb", [128, 1024], F32)
        p.dma("sp", bvb[:], w["bv_brd"], w=["bvb"], tag="ld_bv")
        biases = {"q": (bqb, "bqb"), "k": (bkb, "bkb"), "v": (bvb, "bvb")}
        for j in range(TPS):
            pr = j % 2
            emit_ln_tile(p, ident, pool, j,
                         x_in[128 * (s0 + j):128 * (s0 + j + 1), :],
                         xnf.ap(), ("xnf",))
            for wsb, wkey, bias in ((wq, "wq", "q"), (wk, "wk", "k"), (wv, "wv", "v")):
                ktk = f"qtk{pr}"
                bb, bbk = biases[bias]
                for dg in range(2):
                    ps, kps = qkps[dg], f"qkps{dg}"
                    prods = [(xnf[:, 128 * c:128 * (c + 1)],
                              wsb[:, 1024 * c + 512 * dg:1024 * c + 512 * (dg + 1)],
                              [("xnf",), wkey]) for c in range(8)]
                    p.mm_group(ps[:], prods, kps, tag=f"{bias}{j}")
                    p.tt(qtk[pr][:, 512 * dg:512 * (dg + 1)], ps[:],
                         bb[:, 512 * dg:512 * (dg + 1)], ALU.add,
                         r=[kps, bbk], w=[ktk])
                if bias == "v":
                    p.copy("dve", v_tok[:, 1024 * j:1024 * (j + 1)], qtk[pr][:],
                           r=[ktk], w=[("vtok", j)])
                else:
                    dstb = q_fm if bias == "q" else k_fm
                    dkey = ("qfm", j) if bias == "q" else ("kfm", j)
                    transpose_tile_to(p, ident, pool, pr, qtk[pr], ktk,
                                      fap(dstb, 128 * j, [[T, 8], [1, 128]]), dkey)
        p.pop_scope()
        attn_fm = p.sb("afm", [128, 8 * T], BF16)

        p.push_scope()

        def q_ap(h, t128):
            b = 64 * (h % 2)
            return (q_fm[b:b + 64, 2048 * (h // 2) + 128 * t128:
                         2048 * (h // 2) + 128 * (t128 + 1)], [("qfm", t128)])

        def k_ap(h, kc):
            b = 64 * (h % 2)
            return (k_fm[b:b + 64, 2048 * (h // 2) + 512 * kc:
                         2048 * (h // 2) + 512 * (kc + 1)],
                    [("kfm", 4 * kc + i) for i in range(4)])

        def v_ap(h, kbt):
            return (v_tok[:, 1024 * kbt + 64 * h:1024 * kbt + 64 * (h + 1)],
                    [("vtok", kbt)])

        def attn_write(h, g, pv_ap, kpv, QGW=256):
            b = 64 * (h % 2)
            dst = attn_fm[b:b + 64, 2048 * (h // 2) + QGW * g:
                          2048 * (h // 2) + QGW * (g + 1)]
            p.acopy(dst, pv_ap, r=[kpv], w=[("afm", h, g)])

        emit_attn_core(p, f"{uname}ac{s}_", ident, kind="softmax", suffix=True, QGT=2,
                       nheads=16, q_ap=q_ap, k_ap=k_ap, v_ap=v_ap,
                       out_base=lambda h: 64 * (h % 2), attn_write=attn_write,
                       scale=(scl[:, 0:1], "ascl"), maskv=maskv)
        p.pop_scope()

        p.push_scope()
        wo = load_w_fm(p, "pool", w["WoT"], 1024, "wo")
        bob = p.sb("bob", [128, 1024], F32)
        p.dma("sp", bob[:], w["bo_brd"], w=["bob"], tag="ld_bo")

        def afm_rkeys(c, j):
            g = j // 2
            return [("afm", 2 * c, g), ("afm", 2 * c + 1, g)]

        def epilogue(j, dg, ps_ap, kps, xt, kx, dst, ko):
            p.tt(dst, ps_ap, xt[:, 512 * dg:512 * (dg + 1)], ALU.add,
                 r=[kps, kx], w=[ko])
            p.tt(dst, dst, bob[:, 512 * dg:512 * (dg + 1)], ALU.add,
                 r=[ko, "bob"], w=[ko])

        emit_oproj(p, f"{uname}op{s}_", ident, attn_fm, afm_rkeys, wo, "wo",
                   x_in, x_out, s0, epilogue, QGT=2)
        p.pop_scope()
        p.pop_scope()


def l0_attn_a_host_weights(inputs):
    Wq, Wk, Wv, Wo = (inputs["L0_attn_a_Wq"], inputs["L0_attn_a_Wk"],
                      inputs["L0_attn_a_Wv"], inputs["L0_attn_a_Wo"])
    bq, bk, bv, bo = (inputs["L0_attn_a_bq"], inputs["L0_attn_a_bk"],
                      inputs["L0_attn_a_bv"], inputs["L0_attn_a_bo"])
    diag = np.zeros((128, 128), np.float32)
    qq = np.arange(128)[:, None]
    kk = np.arange(128)[None, :]
    same64 = (qq // 64) == (kk // 64)
    masked = (same64 & (kk > qq)) | ((qq >= 64) & (kk < 64))
    diag[masked] = NEG
    maskv = np.zeros((4 * 128, 512), np.float32)
    for v in range(4):
        maskv[128 * v:128 * (v + 1), 0:128 * v] = NEG
        maskv[128 * v:128 * (v + 1), 128 * v:128 * (v + 1)] = diag
    return {
        "WqT": np.ascontiguousarray(Wq.T), "WkT": np.ascontiguousarray(Wk.T),
        "WvT": np.ascontiguousarray(Wv.T), "WoT": np.ascontiguousarray(Wo.T),
        "bq_brd": np.ascontiguousarray(np.broadcast_to(bq, (128, 1024))),
        "bk_brd": np.ascontiguousarray(np.broadcast_to(bk, (128, 1024))),
        "bv_brd": np.ascontiguousarray(np.broadcast_to(bv, (128, 1024))),
        "bo_brd": np.ascontiguousarray(np.broadcast_to(bo, (128, 1024))),
        "attn_scale_rep": np.ascontiguousarray(
            np.broadcast_to(inputs["L0_attn_a_attn_scale"], (128, 1))),
        "maskv": maskv,
    }


WSPECS_L0_ATTN_A = [("WqT", [1024, 1024]), ("WkT", [1024, 1024]),
                    ("WvT", [1024, 1024]), ("WoT", [1024, 1024]),
                    ("bq_brd", [128, 1024]), ("bk_brd", [128, 1024]),
                    ("bv_brd", [128, 1024]), ("bo_brd", [128, 1024]),
                    ("attn_scale_rep", [128, 1]), ("maskv", [512, 512])]


def build_l0_attn_b(p, ident, w, x_in, x_out, uname=""):
    maskv = []
    for v in range(4):
        m = p.sb("bmask", [128, 512], F32)
        p.dma("sp", m[:], w["maskv01"][v * 128:(v + 1) * 128, :], w=[("bmask", v)],
              tag="ld_mask")
        maskv.append((m[:], ("bmask", v)))

    for s in range(SEQ_PER_CORE):
        p.push_scope()
        q_fm = p.sb("qfm", [128, 8 * T], BF16)
        k_fm = p.sb("kfm", [128, 8 * T], BF16)
        v_tok = p.sb("vtok", [128, TPS * 1024], BF16)
        s0 = s * TPS

        p.push_scope()
        pool = LnTilePool(p, f"{uname}qv{s}_")
        xnf = p.sb("xnf", [128, 8 * 128], BF16)
        qkps = [p.psum("qkps", [128, 512], F32) for _ in range(2)]
        qtk = [p.sb("qtk", [128, 1024], BF16) for _ in range(2)]
        rtmp = [p.sb("rtmp", [128, 512], F32) for _ in range(2)]
        rtmp2 = [p.sb("rtmp2", [128, 512], F32) for _ in range(2)]
        cosb = [p.sb("cosb", [128, 1024], F32) for _ in range(2)]
        sinb = [p.sb("sinb", [128, 1024], F32) for _ in range(2)]
        wq = load_w_fm(p, "pool", w["wq"], 1024, "wq")
        wk = load_w_fm(p, "pool", w["wk"], 1024, "wk")
        wv = load_w_fm(p, "pool", w["wv"], 1024, "wv")
        for j in range(TPS):
            pr = j % 2
            kcos, ksin = f"cosb{pr}", f"sinb{pr}"
            p.dma("sp", cosb[pr][:], w["COSI"][128 * j:128 * (j + 1), :],
                  w=[kcos], tag="ld_cos")
            p.dma("sp", sinb[pr][:], w["SI2"][128 * j:128 * (j + 1), :],
                  w=[ksin], tag="ld_sin")
            emit_ln_tile(p, ident, pool, j,
                         x_in[128 * (s0 + j):128 * (s0 + j + 1), :],
                         xnf.ap(), ("xnf",))
            for wsb, wkey, kindq in ((wq, "wq", "q"), (wk, "wk", "k"), (wv, "wv", "v")):
                ktk = f"qtk{pr}"
                for dg in range(2):
                    ps, kps = qkps[dg], f"qkps{dg}"
                    prods = [(xnf[:, 128 * c:128 * (c + 1)],
                              wsb[:, 1024 * c + 512 * dg:1024 * c + 512 * (dg + 1)],
                              [("xnf",), wkey]) for c in range(8)]
                    p.mm_group(ps[:], prods, kps, tag=f"{kindq}{j}")
                    dst = qtk[pr][:, 512 * dg:512 * (dg + 1)]
                    if kindq == "v":
                        p.copy("dve", dst, ps[:], r=[kps], w=[ktk])
                    else:
                        kr1, kr2 = f"rtmp{dg}", f"rtmp2{dg}"
                        p.tt(rtmp[dg][:], ps[:], cosb[pr][:, 512 * dg:512 * (dg + 1)],
                             ALU.mult, r=[kps, kcos], w=[kr1])
                        swap_ap = AP(ps.ap().tensor, 1,
                                     [list(ps.ap().ap[0]), [2, 256], [-1, 2]])
                        p.tt(rtmp2[dg][:], swap_ap, sinb[pr][:, 512 * dg:512 * (dg + 1)],
                             ALU.mult, r=[kps, ksin], w=[kr2])
                        p.tt(rtmp[dg][:], rtmp[dg][:], rtmp2[dg][:], ALU.add,
                             r=[kr1, kr2], w=[kr1])
                        p.op("dve", lambda e, o=dst, i=rtmp[dg]: e.tensor_relu(
                             out=o, in_=i[:]), r=[kr1], w=[ktk])
                if kindq == "v":
                    p.copy("dve", v_tok[:, 1024 * j:1024 * (j + 1)], qtk[pr][:],
                           r=[ktk], w=[("vtok", j)])
                else:
                    dstb = q_fm if kindq == "q" else k_fm
                    dkey = ("qfm", j) if kindq == "q" else ("kfm", j)
                    transpose_tile_to(p, ident, pool, pr, qtk[pr], ktk,
                                      fap(dstb, 128 * j, [[T, 8], [1, 128]]), dkey)
        p.pop_scope()
        attn_fm = p.sb("afm", [128, 8 * T], BF16)

        p.push_scope()

        def q_ap(h, t128):
            b = 64 * (h % 2)
            return (q_fm[b:b + 64, 2048 * (h // 2) + 128 * t128:
                         2048 * (h // 2) + 128 * (t128 + 1)], [("qfm", t128)])

        def k_ap(h, kc):
            b = 64 * (h % 2)
            return (k_fm[b:b + 64, 2048 * (h // 2) + 512 * kc:
                         2048 * (h // 2) + 512 * (kc + 1)],
                    [("kfm", 4 * kc + i) for i in range(4)])

        def v_ap(h, kbt):
            return (v_tok[:, 1024 * kbt + 64 * h:1024 * kbt + 64 * (h + 1)],
                    [("vtok", kbt)])

        def attn_write(h, g, pv_ap, kpv, QGW=256):
            b = 64 * (h % 2)
            dst = attn_fm[b:b + 64, 2048 * (h // 2) + QGW * g:
                          2048 * (h // 2) + QGW * (g + 1)]
            p.acopy(dst, pv_ap, r=[kpv], w=[("afm", h, g)])

        emit_attn_core(p, f"{uname}bc{s}_", ident, kind="linear", suffix=False, QGT=2,
                       nheads=16, q_ap=q_ap, k_ap=k_ap, v_ap=v_ap,
                       out_base=lambda h: 64 * (h % 2), attn_write=attn_write,
                       scale=None, maskv=maskv)
        p.pop_scope()

        p.push_scope()
        wo = load_w_fm(p, "pool", w["wo"], 1024, "wo")

        def afm_rkeys(c, j):
            g = j // 2
            return [("afm", 2 * c, g), ("afm", 2 * c + 1, g)]

        def epilogue(j, dg, ps_ap, kps, xt, kx, dst, ko):
            p.tt(dst, ps_ap, xt[:, 512 * dg:512 * (dg + 1)], ALU.add,
                 r=[kps, kx], w=[ko])

        emit_oproj(p, f"{uname}op{s}_", ident, attn_fm, afm_rkeys, wo, "wo",
                   x_in, x_out, s0, epilogue, QGT=2)
        p.pop_scope()
        p.pop_scope()


def l0_attn_b_host_weights(inputs):
    d = 64
    inv_freq = 1.0 / (10000.0 ** (np.arange(0, d, 2, dtype=np.float32) / d))
    t = np.arange(T, dtype=np.float32)
    freqs = (t[:, None] * inv_freq[None, :]).astype(np.float32)
    cos = np.cos(freqs.astype(np.float64)).astype(np.float32)
    sin = np.sin(freqs.astype(np.float64)).astype(np.float32)
    cosi = np.repeat(cos, 2, axis=1)
    si2 = np.empty((T, 64), np.float32)
    si2[:, 0::2] = -sin
    si2[:, 1::2] = sin
    COSI = np.tile(cosi, (1, 16))
    SI2 = np.tile(si2, (1, 16))
    tril = np.tril(np.ones((128, 128), np.float32))
    maskv = np.zeros((4 * 128, 512), np.float32)
    for v in range(4):
        maskv[128 * v:128 * (v + 1), 0:128 * v] = 1.0
        maskv[128 * v:128 * (v + 1), 128 * v:128 * (v + 1)] = tril
    return {
        "wq": np.ascontiguousarray(inputs["L0_attn_b_wq"]),
        "wk": np.ascontiguousarray(inputs["L0_attn_b_wk"]),
        "wv": np.ascontiguousarray(inputs["L0_attn_b_wv"]),
        "wo": np.ascontiguousarray(inputs["L0_attn_b_wo"]),
        "COSI": COSI, "SI2": SI2, "maskv01": maskv,
    }


WSPECS_L0_ATTN_B = [("wq", [1024, 1024]), ("wk", [1024, 1024]),
                    ("wv", [1024, 1024]), ("wo", [1024, 1024]),
                    ("COSI", [2048, 1024]), ("SI2", [2048, 1024]),
                    ("maskv01", [512, 512])]


PERM_A = [0, 1, 4, 5, 8, 9, 12, 13, 2, 3, 6, 7, 10, 11, 14, 15]


def build_l1_attn_a(p, ident, w, x_in, x_out, uname=""):
    maskv = []
    for v in range(4):
        m = p.sb("cmask", [128, 512], F32)
        p.dma("sp", m[:], w["maskv"][v * 128:(v + 1) * 128, :], w=[("cmask", v)],
              tag="ld_mask")
        maskv.append((m[:], ("cmask", v)))
    nwb = p.sb("nwb", [128, 1024], F32)
    p.dma("sp", nwb[:], w["normw_brd"], w=["nwb"], tag="ld_nw")

    for s in range(SEQ_PER_CORE):
        p.push_scope()
        q_fm = p.sb("qfm", [128, 8 * T], BF16)
        k_fm = p.sb("kfm", [128, 2 * T], BF16)
        v_tok = p.sb("vtok", [128, TPS * 256], BF16)
        s0 = s * TPS

        p.push_scope()
        pool = LnTilePool(p, f"{uname}qa{s}_", rms=True)
        xnf = p.sb("xnf", [128, 8 * 128], BF16)
        qkps = [p.psum("qkps", [128, 512], F32) for _ in range(2)]
        qtk = [p.sb("qtk", [128, 1024], BF16) for _ in range(2)]
        rfb = [p.sb("rfb", [128, 1024], F32) for _ in range(2)]
        wqp = load_w_fm(p, "pool", w["WqP"], 1024, "wqp")
        wkk = load_w_fm(p, "pool", w["w_k"], 256, "wkk")
        wvv = load_w_fm(p, "pool", w["w_v"], 256, "wvv")
        for j in range(TPS):
            pr = j % 2
            krf = f"rfb{pr}"
            p.dma("sp", rfb[pr][:], w["RF_rep"][128 * j:128 * (j + 1), :],
                  w=[krf], tag="ld_rf")
            emit_ln_tile(p, ident, pool, j,
                         x_in[128 * (s0 + j):128 * (s0 + j + 1), :],
                         xnf.ap(), ("xnf",), normw=(nwb, "nwb"))
            ktk = f"qtk{pr}"
            for dg in range(2):
                ps, kps = qkps[dg], f"qkps{dg}"
                prods = [(xnf[:, 128 * c:128 * (c + 1)],
                          wqp[:, 1024 * c + 512 * dg:1024 * c + 512 * (dg + 1)],
                          [("xnf",), "wqp"]) for c in range(8)]
                p.mm_group(ps[:], prods, kps, tag=f"q{j}")
                p.tt(qtk[pr][:, 512 * dg:512 * (dg + 1)], ps[:],
                     rfb[pr][:, 512 * dg:512 * (dg + 1)], ALU.mult,
                     r=[kps, krf], w=[ktk])
            transpose_tile_to(p, ident, pool, pr, qtk[pr], ktk,
                              fap(q_fm, 128 * j, [[T, 8], [1, 128]]), ("qfm", j))
            for wsb, wkey, kindq in ((wkk, "wkk", "k"), (wvv, "wvv", "v")):
                ps, kps = qkps[0], "qkps0"
                prods = [(xnf[:, 128 * c:128 * (c + 1)],
                          wsb[:, 256 * c:256 * (c + 1)],
                          [("xnf",), wkey]) for c in range(8)]
                p.mm_group(ps[:, 0:256], prods, kps, tag=f"{kindq}{j}")
                if kindq == "k":
                    p.tt(qtk[pr][:, 0:256], ps[:, 0:256], rfb[pr][:, 0:256],
                         ALU.mult, r=[kps, krf], w=[ktk])
                    transpose_tile_to(p, ident, pool, pr, qtk[pr], ktk,
                                      fap(k_fm, 128 * j, [[T, 2], [1, 128]]),
                                      ("kfm", j), nch=2)
                else:
                    p.copy("dve", v_tok[:, 256 * j:256 * (j + 1)], ps[:, 0:256],
                           r=[kps], w=[("vtok", j)])
        p.pop_scope()
        attn_fm = p.sb("afm", [128, 8 * T], BF16)

        p.push_scope()

        def kv_of(pos):
            return (pos % 2) + 2 * (pos // 8)

        def q_ap(h, t128):
            b = 64 * (h % 2)
            return (q_fm[b:b + 64, 2048 * (h // 2) + 128 * t128:
                         2048 * (h // 2) + 128 * (t128 + 1)], [("qfm", t128)])

        def k_ap(h, kc):
            c = kv_of(h)
            b = 64 * (c % 2)
            return (k_fm[b:b + 64, 2048 * (c // 2) + 512 * kc:
                         2048 * (c // 2) + 512 * (kc + 1)],
                    [("kfm", 4 * kc + i) for i in range(4)])

        def v_ap(h, kbt):
            c = kv_of(h)
            return (v_tok[:, 256 * kbt + 64 * c:256 * kbt + 64 * (c + 1)],
                    [("vtok", kbt)])

        def attn_write(h, g, pv_ap, kpv, QGW=256):
            b = 64 * (h % 2)
            dst = attn_fm[b:b + 64, 2048 * (h // 2) + QGW * g:
                          2048 * (h // 2) + QGW * (g + 1)]
            p.acopy(dst, pv_ap, r=[kpv], w=[("afm", h, g)])

        emit_attn_core(p, f"{uname}cc{s}_", ident, kind="softmax", suffix=False, QGT=2,
                       nheads=16, q_ap=q_ap, k_ap=k_ap, v_ap=v_ap,
                       out_base=lambda h: 64 * (h % 2), attn_write=attn_write,
                       scale=0.125, maskv=maskv)
        p.pop_scope()

        p.push_scope()
        wop = load_w_fm(p, "pool", w["WoP"], 1024, "wop")
        lst6 = [p.sb("lst6", [128, 12], F32) for _ in range(2)]
        lmv = [p.sb("lmv", [128, 2], F32) for _ in range(2)]
        lsd = [p.sb("lsd", [128, 1], F32) for _ in range(2)]
        xlnb = [p.sb("xlnb", [128, 1024], F32) for _ in range(2)]

        def afm_rkeys(c, j):
            g = j // 2
            return [("afm", 2 * c, g), ("afm", 2 * c + 1, g)]

        def epilogue(j, dg, ps_ap, kps, xt, kx, dst, ko):
            pr = j % 2
            ks6, kmv, ksd, kxl = (f"l{n}{pr}" for n in ("st6", "mv", "sd", "xlnb"))
            if dg == 0:
                p.op("dve", lambda e, a=lst6[pr], b=xt: e.bn_stats(
                     out=a[:, 0:6], in_=b[:, 0:512]), r=[kx], w=[ks6])
                p.op("dve", lambda e, a=lst6[pr], b=xt: e.bn_stats(
                     out=a[:, 6:12], in_=b[:, 512:1024]), r=[kx], w=[ks6])
                p.op("dve", lambda e, a=lmv[pr], b=lst6[pr]: e.bn_aggr(
                     out=a[:, 0:2], in_=b[:]), r=[ks6], w=[kmv])
                eps_ap, eps_key = p.constf(1e-5)
                p.act(lsd[pr][:], lmv[pr][:, 1:2], AF.Sqrt, bias=eps_ap,
                      r=[kmv, eps_key], w=[ksd])
                p.recip(lsd[pr][:], lsd[pr][:], r=[ksd], w=[ksd])
                p.ts(xlnb[pr][:], xt[:], lmv[pr][:, 0:1], lsd[pr][:, 0:1],
                     ALU.subtract, ALU.mult, r=[kx, kmv, ksd], w=[kxl])
            p.tt(dst, ps_ap, xt[:, 512 * dg:512 * (dg + 1)], ALU.add,
                 r=[kps, kx], w=[ko])
            p.tt(dst, dst, xlnb[pr][:, 512 * dg:512 * (dg + 1)], ALU.add,
                 r=[ko, kxl], w=[ko])

        emit_oproj(p, f"{uname}op{s}_", ident, attn_fm, afm_rkeys, wop, "wop",
                   x_in, x_out, s0, epilogue, QGT=4)
        p.pop_scope()
        p.pop_scope()


def l1_attn_a_host_weights(inputs):
    w_q, w_k, w_v, w_o = (inputs["L1_attn_a_w_q"], inputs["L1_attn_a_w_k"],
                          inputs["L1_attn_a_w_v"], inputs["L1_attn_a_w_o"])
    norm_w = inputs["L1_attn_a_norm_w"]
    WqP = np.concatenate([w_q[:, 64 * h:64 * (h + 1)] for h in PERM_A], axis=1)
    WoP = np.concatenate([w_o[64 * h:64 * (h + 1), :] for h in PERM_A], axis=0)
    d_head = 64
    ii = np.arange(d_head)
    theta = (10000.0 ** (-2.0 * (ii // 2).astype(np.float64) / d_head))
    pos = np.arange(T, dtype=np.float64)
    ang = pos[:, None] * theta[None, :]
    sgn = np.where(np.arange(T) % 2 == 0, -1.0, 1.0)[:, None]
    rf = (np.cos(ang) + np.sin(ang) * sgn).astype(np.float32)
    RF_rep = np.tile(rf, (1, 16))
    qq = np.arange(128)[:, None]
    kk = np.arange(128)[None, :]
    diag = np.where(kk > qq, NEG, 0.0).astype(np.float32)
    maskv = np.zeros((4 * 128, 512), np.float32)
    for v in range(4):
        maskv[128 * v:128 * (v + 1), 128 * v:128 * (v + 1)] = diag
        maskv[128 * v:128 * (v + 1), 128 * (v + 1):] = NEG
    return {
        "WqP": np.ascontiguousarray(WqP),
        "w_k": np.ascontiguousarray(w_k), "w_v": np.ascontiguousarray(w_v),
        "WoP": np.ascontiguousarray(WoP),
        "normw_brd": np.ascontiguousarray(np.broadcast_to(norm_w, (128, 1024))),
        "RF_rep": RF_rep, "maskv": maskv,
    }


WSPECS_L1_ATTN_A = [("WqP", [1024, 1024]), ("w_k", [1024, 256]),
                    ("w_v", [1024, 256]), ("WoP", [1024, 1024]),
                    ("normw_brd", [128, 1024]), ("RF_rep", [2048, 1024]),
                    ("maskv", [512, 512])]


def build_l1_attn_b(p, ident, w, x_in, x_out, uname=""):
    ones_bf = p.sb("ones1", [128, 1], BF16)
    p.op("pool", lambda e, a=ones_bf: e.memset(a[:], 1.0), w=["ones1"])

    for s in range(SEQ_PER_CORE):
        p.push_scope()
        qf_fm = p.sb("qffm", [128, 8 * T], BF16)
        kf_tok = p.sb("kftok", [128, TPS * 1024], BF16)
        v_tok = p.sb("vtok", [128, TPS * 1024], BF16)
        s0 = s * TPS

        p.push_scope()
        pool = LnTilePool(p, f"{uname}qb{s}_")
        xnf = p.sb("xnf", [128, 8 * 128], BF16)
        qkps = [p.psum("qkps", [128, 512], F32) for _ in range(2)]
        qtk = [p.sb("qtk", [128, 1024], BF16) for _ in range(2)]
        rtmp = [p.sb("rtmp", [128, 512], F32) for _ in range(3)]
        cosb = [p.sb("cosb", [128, 1024], F32) for _ in range(2)]
        sinb = [p.sb("sinb", [128, 1024], F32) for _ in range(2)]
        wqkv = p.sb("wqkv", [128, 8 * 3072], BF16)
        p.dma("pool", wqkv[:], w["W_qkv"].rearrange("(c p) d -> p c d", p=128),
              w=["wqkv"], tag="ld_wqkv")
        for j in range(TPS):
            pr = j % 2
            kcos, ksin = f"cosb{pr}", f"sinb{pr}"
            p.dma("sp", cosb[pr][:], w["COS2"][128 * j:128 * (j + 1), :],
                  w=[kcos], tag="ld_cos")
            p.dma("sp", sinb[pr][:], w["SIN2"][128 * j:128 * (j + 1), :],
                  w=[ksin], tag="ld_sin")
            emit_ln_tile(p, ident, pool, j,
                         x_in[128 * (s0 + j):128 * (s0 + j + 1), :],
                         xnf.ap(), ("xnf",))
            for part in range(6):
                ps, kps = qkps[part % 2], f"qkps{part % 2}"
                prods = [(xnf[:, 128 * c:128 * (c + 1)],
                          wqkv[:, 3072 * c + 512 * part:3072 * c + 512 * (part + 1)],
                          [("xnf",), "wqkv"]) for c in range(8)]
                p.mm_group(ps[:], prods, kps, tag=f"qkv{j}")
                kindq = "qkv"[part // 2]
                dgl = part % 2
                ktk = f"qtk{pr}"
                if kindq == "v":
                    p.copy("dve", v_tok[:, 1024 * j + 512 * dgl:
                                        1024 * j + 512 * (dgl + 1)], ps[:],
                           r=[kps], w=[("vtok", j)])
                    continue
                kr0, kr1, kr2 = "rt0", "rt1", "rt2"
                p.tt(rtmp[0][:], ps[:], cosb[pr][:, 512 * dgl:512 * (dgl + 1)],
                     ALU.mult, r=[kps, kcos], w=[kr0])
                hs_ap = AP(ps.ap().tensor, 32,
                           [list(ps.ap().ap[0]), [64, 8], [-32, 2], [1, 32]])
                p.tt(rtmp[1][:], hs_ap, sinb[pr][:, 512 * dgl:512 * (dgl + 1)],
                     ALU.mult, r=[kps, ksin], w=[kr1])
                p.tt(rtmp[0][:], rtmp[0][:], rtmp[1][:], ALU.add,
                     r=[kr0, kr1], w=[kr0])
                sc = 0.125 if kindq == "q" else 1.0
                p.ts(rtmp[1][:], rtmp[0][:], sc, 0.0, ALU.mult, ALU.min,
                     r=[kr0], w=[kr1])
                p.op("act", lambda e, o=rtmp[2], i=rtmp[1]: e.activation(
                     out=o[:], in_=i[:], func=AF.Exp), r=[kr1], w=[kr2])
                p.ts(rtmp[1][:], rtmp[0][:], sc, 0.0, ALU.mult, ALU.max,
                     r=[kr0], w=[kr1])
                p.tt(qtk[pr][:, 512 * dgl:512 * (dgl + 1)], rtmp[1][:],
                     rtmp[2][:], ALU.add, r=[kr1, kr2], w=[ktk])
                if kindq == "q" and dgl == 1:
                    transpose_tile_to(p, ident, pool, pr, qtk[pr], ktk,
                                      fap(qf_fm, 128 * j, [[T, 8], [1, 128]]),
                                      ("qffm", j))
                elif kindq == "k" and dgl == 1:
                    p.copy("dve", kf_tok[:, 1024 * j:1024 * (j + 1)], qtk[pr][:],
                           r=[ktk], w=[("kftok", j)])
        p.pop_scope()
        attn_fm = p.sb("afm2", [128, 8 * T], BF16)

        p.push_scope()
        kvps = [p.psum("kvps", [128, 128], F32) for _ in range(2)]
        kvh = p.sb("kvh", [128, 8 * 65], BF16)
        attn_tok = [p.sb("atok", [128, 1024], BF16) for _ in(0, 1)]
        for h in range(16):
            b = 64 * (h % 2)
            pi = h % 2
            ps, kps = kvps[pi], f"kvps{pi}"
            tp = (0, 64) if b else None
            for kbt in range(TPS):
                p.mm(ps[b:b + 64, 0:64],
                     kf_tok[:, 1024 * kbt + 64 * h:1024 * kbt + 64 * (h + 1)],
                     v_tok[:, 1024 * kbt + 64 * h:1024 * kbt + 64 * (h + 1)],
                     kbt == 0, kbt == TPS - 1,
                     r=[("kftok", kbt), ("vtok", kbt)],
                     w=([kps] if kbt == 0 else ()), a=([kps] if kbt else ()),
                     tag="kv", tile_position=tp)
            for kbt in range(TPS):
                p.mm(ps[b:b + 64, 64:65],
                     kf_tok[:, 1024 * kbt + 64 * h:1024 * kbt + 64 * (h + 1)],
                     ones_bf[:], kbt == 0, kbt == TPS - 1,
                     r=[("kftok", kbt), "ones1"], a=[kps],
                     tag="ksum", tile_position=tp)
            p.acopy(kvh[b:b + 64, 65 * (h // 2):65 * (h // 2) + 65],
                    ps[b:b + 64, 0:65], r=[kps], w=[("kvh", h)])
        ops2 = [p.psum("obps", [128, 128], F32) for _ in (0, 1)]
        dsc = [p.sb("dsc", [128, 1], F32) for _ in (0, 1)]
        tp_pool = PtPool(p, f"{uname}p3{s}_")
        for j in range(TPS):
            pr = j % 2
            katk = f"atok{pr}"
            for h in range(16):
                b = 64 * (h % 2)
                pi = h % 2
                ps, kps = ops2[pi], f"obps{pi}"
                tp = (64, 0) if b else None
                p.mm(ps[:, 0:65],
                     qf_fm[b:b + 64, 2048 * (h // 2) + 128 * j:
                           2048 * (h // 2) + 128 * (j + 1)],
                     kvh[b:b + 64, 65 * (h // 2):65 * (h // 2) + 65],
                     True, True, r=[("qffm", j), ("kvh", h)], w=[kps],
                     tag="ob", tile_position=tp)
                kd = f"dsc{pi}"
                p.op("dve", lambda e, o=dsc[pi], i=ps: e.tensor_scalar_max(
                     out=o[:], in0=i[:, 64:65], scalar1=1e-6), r=[kps], w=[kd])
                p.recip(dsc[pi][:], dsc[pi][:], r=[kd], w=[kd])
                p.acopy(attn_tok[pr][:, 64 * h:64 * (h + 1)], ps[:, 0:64],
                        scale=dsc[pi][:, 0:1], r=[kps, kd], w=[katk])
            transpose_tile_to(p, ident, tp_pool, pr, attn_tok[pr], katk,
                              fap(attn_fm, 128 * j, [[T, 8], [1, 128]]),
                              ("afm2", j))
        p.pop_scope()

        p.push_scope()
        wo = load_w_fm(p, "pool", w["W_out"], 1024, "wo2")

        def afm_rkeys(c, j):
            return [("afm2", j)]

        def epilogue(j, dg, ps_ap, kps, xt, kx, dst, ko):
            p.tt(dst, ps_ap, xt[:, 512 * dg:512 * (dg + 1)], ALU.add,
                 r=[kps, kx], w=[ko])

        emit_oproj(p, f"{uname}op{s}_", ident, attn_fm, afm_rkeys, wo, "wo2",
                   x_in, x_out, s0, epilogue, QGT=4)
        p.pop_scope()
        p.pop_scope()


def l1_attn_b_host_weights(inputs):
    hd = 64
    inv_freq = 1.0 / (500000.0 ** (np.arange(0, hd, 2, dtype=np.float32) / hd))
    t = np.arange(T, dtype=np.float32)
    freqs = (t[:, None] * inv_freq[None, :]).astype(np.float32)
    emb = np.concatenate([freqs, freqs], axis=1)
    cos = np.cos(emb.astype(np.float64)).astype(np.float32)
    sin = np.sin(emb.astype(np.float64)).astype(np.float32)
    sin_sgn = sin.copy()
    sin_sgn[:, 0:32] = -sin[:, 0:32]
    COS2 = np.tile(cos, (1, 16))
    SIN2 = np.tile(sin_sgn, (1, 16))
    return {
        "W_qkv": np.ascontiguousarray(inputs["L1_attn_b_W_qkv"]),
        "W_out": np.ascontiguousarray(inputs["L1_attn_b_W_out"]),
        "COS2": COS2, "SIN2": SIN2,
    }


WSPECS_L1_ATTN_B = [("W_qkv", [1024, 3072]), ("W_out", [1024, 1024]),
                    ("COS2", [2048, 1024]), ("SIN2", [2048, 1024])]


def build_l1_ffn(p, ident, w, x_in, x_out, uname=""):
    GT = 1024
    GTILES = GT // 128
    NG = NTOK // GT

    p.push_scope()
    gw = p.sb("gw", [128, 64], F32)
    p.dma("sp", gw[:], w["gate_wT"].rearrange("(c p) a -> p c a", p=128),
          w=["gw"], tag="ld_gw")
    gwh = p.sb("gwh", [128, 64], BF16)
    gwl = p.sb("gwl", [128, 64], BF16)
    p.copy("dve", gwh[:], gw[:], r=["gw"], w=["gwh"])
    p.tt(gwl[:], gw[:], gwh[:], ALU.subtract, r=["gw", "gwh"], w=["gwl"])

    p.push_scope()
    wsl = [p.sb("wsl", [128, 11264], BF16) for _ in range(2)]
    for g in range(NG):
        p.push_scope()
        xn = p.sb("xn", [128, 8 * GT], BF16)
        comb = p.sb("comb", [128, 8 * GTILES], F32)
        acc = p.sb("acc", [128, 1024 * GTILES], F32)

        p.push_scope()
        xn_lo = p.sb("xnlo", [128, 8 * GT], BF16)
        zps = p.psum("zps", [128, 64], F32)
        z_sb = p.sb("z", [128, 8], F32)
        rsc = router_scratch(p, 8)

        def hook(j, gi, g=g):
            tg = f"{uname}g{g}t{j}"
            prods = []
            for c in range(8):
                xn_c = xn[:, fm_cols(GT, c, 128 * j, 128)]
                lo_c = xn_lo[:, fm_cols(GT, c, 128 * j, 128)]
                gs = slice(8 * c, 8 * (c + 1))
                prods.append((xn_c, gwh[:, gs], [("xnfm", j), "gwh"]))
                prods.append((xn_c, gwl[:, gs], [("xnfm", j), "gwl"]))
                prods.append((lo_c, gwh[:, gs], [("xnlo", j), "gwh"]))
            p.mm_group(zps[:, 0:8], prods, "zps", tag=f"lg{tg}")
            p.copy("dve", z_sb[:], zps[:, 0:8], r=["zps"], w=["zsb"])
            emit_router(p, z_sb, "zsb", 0, 8, 2, comb[:, 8 * j:8 * (j + 1)],
                        ("comb", j), 2.5, rsc, tag=f"{tg}r")

        emit_ln_to_fm(p, ident, lambda gi: x_in[128 * gi:128 * (gi + 1), :],
                      [(j, g * GTILES + j) for j in range(GTILES)],
                      xn, "xnfm", GT, lo_fm=xn_lo, lo_key="xnlo",
                      tag=f"{uname}l1f{g}_", per_tile_hook=hook)
        p.pop_scope()

        p.push_scope()
        g_fm = p.sb("gfm", [128, 11 * GT], BF16)
        eps_ = [p.psum("eps", [128, 512], F32) for _ in range(2)]
        silu_t = [p.sb("silt", [128, 512], F32) for _ in range(2)]
        slot_cnt = [0]

        def wload(name, e, shp):
            sl = slot_cnt[0] % 2
            slot_cnt[0] += 1
            kw = f"ewsl{sl}"
            if name in ("w_gate", "w_up"):
                src = w[name][e].rearrange("(c p) f -> p c f", p=128)
            else:
                src = w[name][e].rearrange("(hc p) d -> p hc d", p=128)
            p.dma("pool", wsl[sl][:], src, w=[kw], tag=f"ld{name}{e}",
                  nofence=(slot_cnt[0] > 2))
            return wsl[sl], kw

        for e in range(8):
            wgs, kwg = wload("w_gate", e, None)
            wus, kwu = wload("w_up", e, None)
            for hc in range(11):
                for tg in range(GT // 512):
                    gps, kgps = eps_[0], "eps0"
                    ups, kups = eps_[1], "eps1"
                    gp = [(wgs[:, 1408 * c + 128 * hc:1408 * c + 128 * (hc + 1)],
                           xn[:, fm_cols(GT, c, 512 * tg, 512)],
                           [kwg] + [("xnfm", 4 * tg + i) for i in range(4)])
                          for c in range(8)]
                    p.mm_group(gps[:], gp, kgps, tag=f"ge{e}")
                    up = [(wus[:, 1408 * c + 128 * hc:1408 * c + 128 * (hc + 1)],
                           xn[:, fm_cols(GT, c, 512 * tg, 512)],
                           [kwu] + [("xnfm", 4 * tg + i) for i in range(4)])
                          for c in range(8)]
                    p.mm_group(ups[:], up, kups, tag=f"ue{e}")
                    ksl = f"silt{tg % 2}"
                    p.op("act", lambda e2, o=silu_t[tg % 2], i=gps: e2.activation(
                         out=o[:], in_=i[:], func=AF.Silu), r=[kgps], w=[ksl])
                    p.tt(g_fm[:, fm_cols(GT, 0, 0, 0).start + GT * hc + 512 * tg:
                              GT * hc + 512 * (tg + 1)],
                         silu_t[tg % 2][:], ups[:], ALU.mult,
                         r=[ksl, kups], w=[("gfm", hc, tg)])
            wds, kwd = wload("w_down", e, None)
            for j in range(GTILES):
                for dg in range(2):
                    pi = (2 * j + dg) % 2
                    ps, kps = eps_[pi], f"eps{pi}"
                    prods = [(g_fm[:, GT * hc + 128 * j:GT * hc + 128 * (j + 1)],
                              wds[:, 1024 * hc + 512 * dg:1024 * hc + 512 * (dg + 1)],
                              [("gfm", hc, j // 4), kwd]) for hc in range(11)]
                    p.mm_group(ps[:], prods, kps, tag=f"de{e}")
                    dst = acc[:, 1024 * j + 512 * dg:1024 * j + 512 * (dg + 1)]
                    kacc = ("acc", j, dg)
                    cs = comb[:, 8 * j + e:8 * j + e + 1]
                    if e == 0:
                        p.ts(dst, ps[:], cs, None, ALU.mult, ALU.bypass,
                             r=[kps, ("comb", j)], w=[kacc])
                    else:
                        p.stt(dst, ps[:], cs, dst, ALU.mult, ALU.add,
                              r=[kps, ("comb", j), kacc], w=[kacc])
        xt2 = [p.sb("xt2", [128, 1024], F32) for _ in range(2)]
        for j in range(GTILES):
            pr = j % 2
            kx = f"xt2{pr}"
            gi = g * GTILES + j
            p.dma("sp", xt2[pr][:], x_in[128 * gi:128 * (gi + 1), :], w=[kx],
                  tag="ldx2")
            for dg in range(2):
                dst = acc[:, 1024 * j + 512 * dg:1024 * j + 512 * (dg + 1)]
                p.tt(dst, dst, xt2[pr][:, 512 * dg:512 * (dg + 1)], ALU.add,
                     r=[("acc", j, dg), kx], w=[("acc", j, dg)])
            p.dma("sp", x_out[128 * gi:128 * (gi + 1), :],
                  acc[:, 1024 * j:1024 * (j + 1)],
                  r=[("acc", j, 0), ("acc", j, 1)], tag="stx")
        p.pop_scope()
        p.pop_scope()
    p.pop_scope()

    p.push_scope()
    swg = p.sb("swg", [128, 8 * 2816], BF16)
    swu = p.sb("swu", [128, 8 * 2816], BF16)
    swd = p.sb("swd", [128, 22 * 1024], BF16)
    p.dma("pool", swg[:], w["sw_gate"].rearrange("(c p) f -> p c f", p=128),
          w=["swg"], tag="ld_swg")
    p.dma("pool", swu[:], w["sw_up"].rearrange("(c p) f -> p c f", p=128),
          w=["swu"], tag="ld_swu")
    p.dma("pool", swd[:], w["sw_down"].rearrange("(hc p) d -> p hc d", p=128),
          w=["swd"], tag="ld_swd")
    pool = LnTilePool(p, f"{uname}sw_")
    xnf2 = p.sb("xnf2", [128, 8 * 256], BF16)
    gbuf = p.sb("gbuf", [128, 22 * 256], BF16)
    gups = [p.psum("gups", [128, 256], F32) for _ in range(2)]
    dps = [p.psum("dps", [128, 512], F32) for _ in range(4)]
    silu_t = [p.sb("silt2", [128, 256], F32)]
    xo = [p.sb("xo", [128, 1024], F32) for _ in range(2)]
    for tp2 in range(NTOK // 256):
        for jj in range(2):
            gi = 2 * tp2 + jj
            emit_ln_tile(p, ident, pool, gi,
                         x_in[128 * gi:128 * (gi + 1), :],
                         fap(xnf2, 128 * jj, [[256, 8], [1, 128]]),
                         ("xnf2", jj))
        for hc in range(22):
            gps, kgps = gups[0], "gups0"
            ups, kups = gups[1], "gups1"
            gp = [(swg[:, 2816 * c + 128 * hc:2816 * c + 128 * (hc + 1)],
                   xnf2[:, 256 * c:256 * (c + 1)],
                   ["swg", ("xnf2", 0), ("xnf2", 1)]) for c in range(8)]
            p.mm_group(gps[:], gp, kgps, tag="swg")
            up = [(swu[:, 2816 * c + 128 * hc:2816 * c + 128 * (hc + 1)],
                   xnf2[:, 256 * c:256 * (c + 1)],
                   ["swu", ("xnf2", 0), ("xnf2", 1)]) for c in range(8)]
            p.mm_group(ups[:], up, kups, tag="swu")
            p.op("act", lambda e2, o=silu_t[0], i=gps: e2.activation(
                 out=o[:], in_=i[:], func=AF.Silu), r=[kgps], w=["silt2"])
            p.tt(gbuf[:, 256 * hc:256 * (hc + 1)], silu_t[0][:], ups[:],
                 ALU.mult, r=["silt2", kups], w=[("gbuf", hc)])
            for jj in range(2):
                for dg in range(2):
                    di = 2 * jj + dg
                    p.mm(dps[di][:], gbuf[:, 256 * hc + 128 * jj:
                                          256 * hc + 128 * (jj + 1)],
                         swd[:, 1024 * hc + 512 * dg:1024 * hc + 512 * (dg + 1)],
                         hc == 0, hc == 21,
                         r=[("gbuf", hc), "swd"],
                         w=([f"dps{di}"] if hc == 0 else ()),
                         a=([f"dps{di}"] if hc > 0 else ()), tag="swd")
        for jj in range(2):
            gi = 2 * tp2 + jj
            pr = jj
            kxo = f"xo{pr}"
            p.dma("sp", xo[pr][:], x_out[128 * gi:128 * (gi + 1), :], w=[kxo],
                  tag="ldxo")
            for dg in range(2):
                di = 2 * jj + dg
                p.tt(xo[pr][:, 512 * dg:512 * (dg + 1)],
                     xo[pr][:, 512 * dg:512 * (dg + 1)], dps[di][:], ALU.add,
                     r=[kxo, f"dps{di}"], w=[kxo])
            p.dma("sp", x_out[128 * gi:128 * (gi + 1), :], xo[pr][:],
                  r=[kxo], tag="stxo")
    p.pop_scope()
    p.pop_scope()


def l1_ffn_host_weights(inputs):
    return {
        "gate_wT": np.ascontiguousarray(inputs["L1_ffn_gate_w"].T),
        "w_gate": np.ascontiguousarray(inputs["L1_ffn_w_gate"]),
        "w_up": np.ascontiguousarray(inputs["L1_ffn_w_up"]),
        "w_down": np.ascontiguousarray(inputs["L1_ffn_w_down"]),
        "sw_gate": np.ascontiguousarray(inputs["L1_ffn_sw_gate"]),
        "sw_up": np.ascontiguousarray(inputs["L1_ffn_sw_up"]),
        "sw_down": np.ascontiguousarray(inputs["L1_ffn_sw_down"]),
    }


WSPECS_L1_FFN = [("gate_wT", [1024, 8]), ("w_gate", [8, 1024, 1408]),
                 ("w_up", [8, 1024, 1408]), ("w_down", [8, 1408, 1024]),
                 ("sw_gate", [1024, 2816]), ("sw_up", [1024, 2816]),
                 ("sw_down", [2816, 1024])]



SUBLAYER_DEFS = [
    (build_l0_attn_a, WSPECS_L0_ATTN_A, l0_attn_a_host_weights),
    (build_l0_ffn, WSPECS_L0_FFN, l0_ffn_host_weights),
    (build_l0_attn_b, WSPECS_L0_ATTN_B, l0_attn_b_host_weights),
    (build_l0_ffn, WSPECS_L0_FFN, l0_ffn_host_weights),
    (build_l1_attn_a, WSPECS_L1_ATTN_A, l1_attn_a_host_weights),
    (build_l1_ffn, WSPECS_L1_FFN, l1_ffn_host_weights),
    (build_l1_attn_b, WSPECS_L1_ATTN_B, l1_attn_b_host_weights),
    (build_l1_ffn, WSPECS_L1_FFN, l1_ffn_host_weights),
]


def build_sublayer_nc(builder, wspecs):
    nc = bass.Bass("TRN2", target_bir_lowering=False)
    x_in = nc.dram_tensor("x_in", [NTOK, D], F32, kind="ExternalInput").ap()
    x_out = nc.dram_tensor("x_out", [NTOK, D], F32, kind="ExternalOutput").ap()
    w = {name: nc.dram_tensor(name, shape, F32, kind="ExternalInput").ap()
         for name, shape in wspecs}
    w["c_ident"] = nc.dram_tensor("c_ident", [128, 128], BF16,
                                  kind="ExternalInput").ap()
    p = Prog(nc)
    ident = load_ident(p, w)
    builder(p, ident, w, x_in, x_out)
    n = p.emit_all()
    return nc, n


_NC_CACHE = {}


def kernel(**inputs):
    x = np.ascontiguousarray(inputs["L0_attn_a_x"].reshape(16, 2048, 1024))
    shards = [np.ascontiguousarray(x[2 * c:2 * c + 2].reshape(NTOK, D))
              for c in range(N_CORES)]
    ident = np.eye(128, dtype=BF16NP)
    for si, (builder, wspecs, hostw) in enumerate(SUBLAYER_DEFS):
        ck = builder.__name__
        if ck not in _NC_CACHE:
            _NC_CACHE[ck] = build_sublayer_nc(builder, wspecs)
        nc, _ = _NC_CACHE[ck]
        warr = hostw(inputs)
        warr["c_ident"] = ident
        in_maps = [dict(warr, x_in=shards[c]) for c in range(N_CORES)]
        res = run_bass_kernel_spmd(nc, in_maps, core_ids=list(range(N_CORES)))
        shards = [np.ascontiguousarray(res.results[c]["x_out"])
                  for c in range(N_CORES)]
    out = np.stack([s.reshape(2, 2048, 1024) for s in shards])
    return out.reshape(16, 2048, 1024)


def build_fused_nc():
    nc = bass.Bass("TRN2", target_bir_lowering=False)
    x_in = nc.dram_tensor("x_in", [NTOK, D], F32, kind="ExternalInput").ap()
    x_out = nc.dram_tensor("x_out", [NTOK, D], F32, kind="ExternalOutput").ap()
    t0 = nc.dram_tensor("xs0", [NTOK, D], F32, kind="Internal").ap()
    t1 = nc.dram_tensor("xs1", [NTOK, D], F32, kind="Internal").ap()
    ident_d = nc.dram_tensor("c_ident", [128, 128], BF16, kind="ExternalInput").ap()
    wgroups = {}
    for builder, wspecs, hostw in SUBLAYER_DEFS:
        bn = builder.__name__
        if bn not in wgroups:
            w = {name: nc.dram_tensor(f"{bn}_{name}", shape, F32,
                                      kind="ExternalInput").ap()
                 for name, shape in wspecs}
            w["c_ident"] = ident_d
            wgroups[bn] = w
    p = Prog(nc)
    ident = load_ident(p, {"c_ident": ident_d})
    chain = [x_in, t0, t1, t0, t1, t0, t1, t0, x_out]
    for si, (builder, wspecs, hostw) in enumerate(SUBLAYER_DEFS):
        p.push_scope()
        builder(p, ident, wgroups[builder.__name__], chain[si], chain[si + 1],
                uname=f"s{si}")
        p.pop_scope()
    n = p.emit_all()
    return nc, n


_FUSED_CACHE = []


def kernel_fused(**inputs):
    x = np.ascontiguousarray(inputs["L0_attn_a_x"].reshape(16, 2048, 1024))
    shards = [np.ascontiguousarray(x[2 * c:2 * c + 2].reshape(NTOK, D))
              for c in range(N_CORES)]
    if not _FUSED_CACHE:
        _FUSED_CACHE.append(build_fused_nc())
    nc, _ = _FUSED_CACHE[0]
    warr = {"c_ident": np.eye(128, dtype=BF16NP)}
    seen = set()
    for builder, wspecs, hostw in SUBLAYER_DEFS:
        bn = builder.__name__
        if bn in seen:
            continue
        seen.add(bn)
        for name, arr in hostw(inputs).items():
            warr[f"{bn}_{name}"] = arr
    in_maps = [dict(warr, x_in=shards[c]) for c in range(N_CORES)]
    res = run_bass_kernel_spmd(nc, in_maps, core_ids=list(range(N_CORES)))
    out = np.stack([res.results[c]["x_out"].reshape(2, 2048, 1024)
                    for c in range(N_CORES)])
    return out.reshape(16, 2048, 1024)


kernel_unfused = kernel
kernel = kernel_fused
